# version 10
# baseline (speedup 1.0000x reference)
import numpy as np
from contextlib import ExitStack

import concourse.bass as bass
import concourse.mybir as mybir
from concourse.bass import ds
from concourse.bass_utils import run_bass_kernel_spmd
from concourse.bacc import Bacc
from concourse.replica_groups import maybe_share_collective_output_space

F32 = mybir.dt.float32
I32 = mybir.dt.int32
BF16 = mybir.dt.bfloat16
AF = mybir.ActivationFunctionType
ALU = mybir.AluOpType
AX = mybir.AxisListType.X

NCORES = 8
T = 16384
D = 1024
DC = 8
KD = 64
HD = 512
TCORE = 2048
PW = 512
NPAN = 4
NCH = 16
NEG = -1.0e9

DEBUG = False


class Prog:

    ENGS = ("sy", "g", "t", "s", "v")

    def __init__(self, nc, ctx):
        self.nc = nc
        self.q = {e: [] for e in self.ENGS}
        self.sems = {}
        for name in ("dx", "dw", "t", "s", "v", "g"):
            self.sems[name] = ctx.enter_context(nc.semaphore("sem_" + name))
        self.counts = {k: 0 for k in self.sems}
        self.waited = {}

    def emit(self, eng, fn, sem=None, amt=1):
        ev = None
        if sem is not None:
            self.counts[sem] += amt
            val = self.counts[sem]
            ev = (sem, val, eng)
            s = self.sems[sem]

            def wrapped(e, fn=fn, s=s, amt=amt):
                inst = fn(e)
                inst.then_inc(s, amt)
            self.q[eng].append(wrapped)
        else:
            self.q[eng].append(fn)
        return ev

    def dma(self, eng, out, in_, **kw):
        sem = "dw" if eng == "g" else "dx"
        return self.emit(eng, lambda e: e.dma_start(out=out, in_=in_, **kw),
                         sem=sem, amt=16)

    def wait(self, eng, ev):
        if ev is None:
            return
        sem, val, prod = ev
        key = (eng, sem)
        if self.waited.get(key, 0) >= val:
            return
        s = self.sems[sem]
        self.q[eng].append(lambda e, s=s, val=val: e.wait_ge(s, val))
        self.waited[key] = val

    def waits(self, eng, *evs):
        for ev in evs:
            self.wait(eng, ev)

    def barrier_all(self):
        evs = []
        for eng, sem in (("t", "t"), ("s", "s"), ("v", "v"), ("g", "g")):
            evs.append(self.emit(eng, lambda e: e.nop(), sem=sem))
        for eng in self.ENGS:
            for ev in evs:
                self.wait(eng, ev)
        return evs

    def flush(self, blk):
        for attr, eng in (("sync", "sy"), ("gpsimd", "g"), ("tensor", "t"),
                          ("scalar", "s"), ("vector", "v")):
            thunks = self.q[eng]

            def runner(e, thunks=thunks):
                for f in thunks:
                    f(e)
            getattr(blk, attr)(runner)

    def final_wait_all_dma(self):
        for eng in ("sy",):
            for sem in ("dx", "dw"):
                s, val = self.sems[sem], self.counts[sem]
                if val:
                    self.q[eng].append(lambda e, s=s, val=val: e.wait_ge(s, val))


def build_nc(upto_phase=5, debug=False, enddump=False):
    nc = Bacc("TRN2", target_bir_lowering=False, debug=False)
    dt = nc.dram_tensor

    x_in = dt("x_in", [128, DC, TCORE], F32, kind="ExternalInput").ap()
    wq_in = dt("wq_in", [128, DC, KD], F32, kind="ExternalInput").ap()
    wk_in = dt("wk_in", [128, DC, KD], F32, kind="ExternalInput").ap()
    wv1_in = dt("wv1_in", [128, DC, KD], F32, kind="ExternalInput").ap()
    wv2_in = dt("wv2_in", [KD, D], F32, kind="ExternalInput").ap()
    routw_in = dt("routw_in", [128, DC, 8], F32, kind="ExternalInput").ap()
    eb_in = dt("eb_in", [8, 1], F32, kind="ExternalInput").ap()
    sw1_in = dt("sw1_in", [128, DC, HD], F32, kind="ExternalInput").ap()
    sb1_in = dt("sb1_in", [128, 4], F32, kind="ExternalInput").ap()
    sw2_in = dt("sw2_in", [128, 4, D], F32, kind="ExternalInput").ap()
    b2_in = dt("b2_in", [9, DC, 128], F32, kind="ExternalInput").ap()
    rw1_in = dt("rw1_in", [8, 128, DC, HD], F32, kind="ExternalInput").ap()
    rb1_in = dt("rb1_in", [8, 128, 4], F32, kind="ExternalInput").ap()
    rw2_in = dt("rw2_in", [8, 128, 4, D], F32, kind="ExternalInput").ap()
    wmod_in = dt("wmod_in", [128, DC, 768], F32, kind="ExternalInput").ap()
    cvec_in = dt("cvec_in", [1, D], F32, kind="ExternalInput").ap()
    btab_in = dt("btab_in", [128, 2 * NCH], F32, kind="ExternalInput").ap()
    jtab_in = dt("jtab_in", [1, 60], I32, kind="ExternalInput").ap()
    wtab_in = dt("wtab_in", [65, 16], F32, kind="ExternalInput").ap()
    wtbb_in = dt("wtbb_in", [65, 16], F32, kind="ExternalInput").ap()
    y_out = dt("y_out", [128, DC, TCORE], F32, kind="ExternalOutput").ap()

    rg = [[0, 1, 2, 3, 4, 5, 6, 7]]
    asp = maybe_share_collective_output_space("AllGather", rg, prefer_local=True)
    ccm_in = dt("ccm_in", [6, 128], F32, kind="Internal").ap()
    ccm_out = dt("ccm_out", [NCORES, 6, 128], F32, kind="Internal",
                 addr_space=asp).ap()
    NKU = KD * TCORE + 128 * 1024
    cku_in = dt("cku_in", [NKU], BF16, kind="Internal").ap()
    cku_out = dt("cku_out", [NCORES, NKU], BF16, kind="Internal",
                 addr_space=asp).ap()
    ccd1_in = dt("ccd1_in", [1, 8], F32, kind="Internal").ap()
    ccd1_out = dt("ccd1_out", [NCORES, 1, 8], F32, kind="Internal",
                  addr_space=asp).ap()
    ccd2_in = dt("ccd2_in", [1, 8], F32, kind="Internal").ap()
    ccd2_out = dt("ccd2_out", [NCORES, 1, 8], F32, kind="Internal",
                  addr_space=asp).ap()
    gcd = dt("gcd", [2, D], F32, kind="Internal").ap()

    dbg = {}
    if debug:
        dbg["mod"] = dt("dbg_mod", [1, 6 * D], F32, kind="ExternalOutput").ap()
        dbg["q"] = dt("dbg_q", [KD, TCORE], F32, kind="ExternalOutput").ap()
        dbg["k"] = dt("dbg_k", [KD, T], F32, kind="ExternalOutput").ap()
        dbg["u"] = dt("dbg_u", [128, NCH * 512], F32, kind="ExternalOutput").ap()
        dbg["x1"] = dt("dbg_x1", [128, DC, TCORE], F32, kind="ExternalOutput").ap()
        dbg["h2"] = dt("dbg_h2", [128, DC, TCORE], F32, kind="ExternalOutput").ap()
        dbg["cf"] = dt("dbg_cf", [9, TCORE], F32, kind="ExternalOutput").ap()
        dbg["r1"] = dt("dbg_r1", [2, TCORE], F32, kind="ExternalOutput").ap()

    ctx = ExitStack()
    sb = lambda name, shape, dtp: nc.alloc_sbuf_tensor(name, shape, dtp).ap()
    psum = lambda name, shape, dtp=F32: nc.alloc_psum_tensor(name, shape, dtp).ap()

    x_fm = sb("x_fm", [128, DC, TCORE], F32)
    q_loc = sb("q_loc", [KD, NPAN, PW], BF16)
    k_loc = sb("k_loc", [KD, TCORE], BF16)
    u_tm_loc = sb("u_tm_loc", [128, 16, KD], BF16)
    wq_s = sb("wq_s", [128, DC, KD], BF16)
    wk_s = sb("wk_s", [128, DC, KD], BF16)
    wv1_s = sb("wv1_s", [128, DC, KD], BF16)
    wv2_s = sb("wv2_s", [KD, D], BF16)
    routw_s = sb("routw_s", [128, DC, 8], BF16)
    eb_s = sb("eb_s", [8, 1], F32)
    btab_s = sb("btab_s", [128, 2 * NCH], F32)
    jtab_s = sb("jtab_s", [1, 60], I32)
    wtab_s = sb("wtab_s", [65, 16], F32)
    wtab_b = sb("wtab_b", [65, 16], F32)
    ident = sb("ident", [128, 128], F32)
    ones_col = sb("ones_col", [128, 1], BF16)
    ones_row = sb("ones_row", [1, 128], BF16)
    sel9 = sb("sel9", [9, 9, 128], BF16)
    one_1x1 = sb("one_1x1", [1, 1], BF16)
    one_1x1f = sb("one_1x1f", [1, 1], F32)
    mone_row = sb("mone_row", [1, PW], BF16)
    g1_bf = sb("g1_bf", [1, D], BF16)
    sa_bf = sb("sa_bf", [1, D], BF16)
    g2_bf = sb("g2_bf", [1, D], BF16)
    sbv_bf = sb("sbv_bf", [1, D], BF16)
    ga_col = sb("ga_col", [128, DC], F32)
    gb_col = sb("gb_col", [128, DC], F32)
    att_f = sb("att_f", [65, PW], F32)
    warm = sb("warm", [1, 8], F32)
    att_b = sb("att_b", [65, PW], BF16)
    rec_p0 = sb("rec_p0", [1, PW], BF16)
    rb_s = sb("rb_s", [KD, PW], BF16)
    o64n = sb("o64n", [KD, PW], BF16)
    u_fm_s = sb("u_fm_s", [KD, PW], F32)

    POOLB = 102464
    pool_base, _pool_end = nc.bump_sbuf(POOLB + 32)
    pool_base = (pool_base + 31) & ~31
    _pvn = [0]

    def pv(off, p, n, dtp, inner=None):
        _pvn[0] += 1
        shape = [p, n] if inner is None else [p, n // inner, inner]
        h = nc.alloc_sbuf_tensor_at(f"pv{_pvn[0]}", shape, dtp,
                                    offset=pool_base + off)
        return h.ap()

    wmod_s = pv(0, 128, DC * 768, F32, 768)
    mod_row = pv(24576, 1, 6 * D, F32)
    msc_row = pv(49152, 1, 2 * D, F32)
    sc_f = pv(57344, 1, D, F32)
    sc_col = pv(61440, 128, DC, F32)

    k_gath = pv(0, KD, T, BF16, 1024)
    u_gath = pv(32768, 128, NCH * 512, BF16, 512)
    p_t = pv(58432, 128, 4 * PW, BF16, PW)
    p2_t = pv(58432, 128, 2 * 2 * PW, BF16, 2 * PW)
    uslot = pv(51200, 128, 2 * 8 * 65, BF16, 8 * 65)
    uslot_d = pv(53280, 128, 8 * 65, BF16)
    masks = pv(54336, 128, 4 * PW, BF16, PW)

    h2 = pv(0, 128, DC * TCORE, BF16, TCORE)
    FB = 32768
    w1s = [pv(FB + i * 16896, 128, DC * HD, BF16, HD) for i in range(2)]
    w2s = [pv(FB + 8192 + i * 16896, 128, 4 * D, BF16, D) for i in range(2)]
    b1s = [pv(FB + 16384 + i * 16896, 128, 4, F32) for i in range(2)]
    FB2 = FB + 2 * 16896
    hdn = pv(FB2, 128, 4 * TCORE, BF16, TCORE)
    cfb_t = pv(FB2 + 16384, 128, 4 * PW, BF16, PW)
    cf_rows = pv(FB2 + 20480, 9, TCORE, BF16)
    act_tm = pv(FB2 + 24576, 128, 16 * 8, F32, 8)
    cf_tm = pv(FB2 + 25088, 128, 16 * 8, F32, 8)
    tcols = pv(FB2 + 25600, 128, 8, F32)
    rt_s = pv(FB2 + 25632, 8, TCORE, F32)
    b2_s = pv(FB2 + 33824, 9, DC * 128, BF16)
    assert FB2 + 33824 + DC * 128 * 2 <= POOLB

    o64acc = pv(66560, 65, NPAN * PW, F32, PW)
    ks_slot = pv(74752, 64, 2 * 1024, BF16, 1024)
    qs_slot = pv(78848, 64, 2 * 2 * PW, BF16, PW)

    LNS = FB2
    xbf_t = pv(LNS, 128, 3 * PW, BF16, PW)
    xsq_t = pv(LNS + 3072, 128, 3 * PW, BF16, PW)
    h1_t = pv(LNS + 6144, 128, 3 * PW, BF16, PW)
    t_f32 = pv(LNS + 9216, 128, 2 * PW, F32, PW)
    rows_f = pv(LNS + 13312, 1, 4 * PW, F32, PW)
    rows_b = pv(LNS + 21504, 1, 4 * PW, BF16, PW)
    assert LNS + 21504 + 4 * PW * 2 <= FB2 + 25632

    B01 = psum("bank01", [128, 2 * PW])
    B56 = psum("bank56", [128, 2 * PW])
    B = [None] * 8
    B[0] = B01[:, 0:PW]
    B[1] = B01[:, PW:2 * PW]
    B[5] = B56[:, 0:PW]
    B[6] = B56[:, PW:2 * PW]
    for i in (2, 3, 4, 7):
        B[i] = psum(f"bank{i}", [128, PW])[:, :]

    P = Prog(nc, ctx)
    E = {}

    with ctx:
        blk = ctx.enter_context(nc.Block())

        E["x"] = P.dma("sy", x_fm[:, :, :], x_in)
        E["btab"] = P.dma("sy", btab_s[:, :], btab_in)
        E["jtab"] = P.dma("sy", jtab_s[:, :], jtab_in)
        E["wtab"] = P.dma("sy", wtab_s[:, :], wtab_in)
        E["wtbb"] = P.dma("sy", wtab_b[:, :], wtbb_in)
        E["eb"] = P.dma("sy", eb_s[:, :], eb_in)
        E["cvec"] = P.dma("sy", att_f[0:1, 0:PW], cvec_in[:, 0:PW])
        E["cvec2"] = P.dma("sy", att_f[32:33, 0:PW], cvec_in[:, PW:D])

        E["wmod"] = P.dma("g", wmod_s, wmod_in)
        E["wq"] = P.dma("g", wq_s[:, :, :], wq_in)
        E["wk"] = P.dma("g", wk_s[:, :, :], wk_in)
        E["wv1"] = P.dma("g", wv1_s[:, :, :], wv1_in)
        E["wv2"] = P.dma("g", wv2_s[:, :], wv2_in)
        E["routw"] = P.dma("g", routw_s[:, :, :], routw_in)

        def consts(g):
            g.memset(ones_col[:, :], 1.0)
            g.memset(ones_row[:, :], 1.0)
            g.memset(one_1x1[:, :], 1.0)
            g.memset(one_1x1f[:, :], 1.0)
            g.memset(mone_row[:, :], -1.0)
            g.memset(ident[:, :], 0.0)
            g.memset(sel9[:, :, :], 0.0)
            return g.drain()
        E["consts0"] = P.emit("g", consts, sem="g")
        P.wait("g", E["consts0"])

        def consts1(g):
            g.affine_select(out=ident[:, :], in_=ident[:, :],
                            compare_op=ALU.not_equal, fill=1.0, base=0,
                            pattern=[[-1, 128]], channel_multiplier=1)
            g.affine_select(out=sel9[:, :, :], in_=sel9[:, :, :],
                            compare_op=ALU.not_equal, fill=1.0, base=0,
                            pattern=[[-1, 9], [0, 128]],
                            channel_multiplier=1)
            return g.drain()
        E["consts"] = P.emit("g", consts1, sem="g")

        P.wait("s", E["consts0"])

        def warmup(s):
            for fn in (AF.Silu, AF.Square, AF.Sqrt, AF.Exp, AF.Ln):
                inst = s.activation(warm[0:1, 0:1], one_1x1f[0:1, 0:1], fn)
            return inst
        E["warm"] = P.emit("s", warmup, sem="s")

        P.waits("s", E["cvec"], E["cvec2"])
        E["scrow"] = P.emit(
            "s", lambda s: (s.activation(sc_f[0:1, 0:PW], att_f[0:1, 0:PW], AF.Silu),
                            s.activation(sc_f[0:1, PW:D], att_f[32:33, 0:PW], AF.Silu))[-1],
            sem="s")

        P.waits("t", E["scrow"], E["consts"])
        for c in range(DC):
            ev = P.emit("t", lambda t, c=c: t.matmul(
                B[0][:, c:c + 1], sc_f[0:1, 128 * c:128 * (c + 1)],
                one_1x1f[:, :], start=True, stop=True, skip_group_check=True),
                sem="t" if c == DC - 1 else None)
        E["sccol_ps"] = ev
        P.wait("v", E["sccol_ps"])
        E["sccol"] = P.emit("v", lambda v: v.tensor_copy(sc_col[:, :], B[0][:, 0:DC]),
                            sem="v")

        P.waits("t", E["sccol"], E["wmod"])
        for c in range(DC):
            P.emit("t", lambda t, c=c: t.matmul(
                B[1][0:1, 0:512], sc_col[:, c:c + 1], wmod_s[:, c, 0:512],
                start=(c == 0), stop=(c == DC - 1), skip_group_check=True))
            ev = P.emit("t", lambda t, c=c: t.matmul(
                B[2][0:1, 0:256], sc_col[:, c:c + 1], wmod_s[:, c, 512:768],
                start=(c == 0), stop=(c == DC - 1), skip_group_check=True),
                sem="t" if c == DC - 1 else None)
        E["modsl_ps"] = ev
        P.wait("v", E["modsl_ps"])
        E["modsl"] = P.emit(
            "v", lambda v: (v.tensor_copy(msc_row[0:1, 0:512], B[1][0:1, 0:512]),
                            v.tensor_copy(msc_row[0:1, 512:768], B[2][0:1, 0:256]))[-1],
            sem="v")

        P.wait("sy", E["modsl"])
        E["modst"] = P.dma("sy", ccm_in.rearrange("j p -> (j p)")[None, :],
                           msc_row[0:1, 0:768])
        P.wait("g", E["modst"])
        E["dummy1st"] = P.dma("g", ccd1_in, warm[0:1, 0:8])
        P.wait("g", E["dummy1st"])
        E["modag"] = P.emit("g", lambda g: g.collective_compute(
            "AllGather", ALU.bypass, replica_groups=rg,
            ins=[ccm_in], outs=[ccm_out]), sem="g")
        E["dummy1"] = P.emit("g", lambda g: g.collective_compute(
            "AllGather", ALU.bypass, replica_groups=rg,
            ins=[ccd1_in], outs=[ccd1_out]), sem="g")
        P.wait("sy", E["dummy1"])
        E["modrow"] = P.dma("sy", mod_row[0:1, :],
                            ccm_out.rearrange("r j p -> (r j p)")[None, :])

        def mod_mid():
            P.wait("v", E["modrow"])

            def modrows(v):
                v.tensor_scalar(g1_bf[0:1, :], mod_row[0:1, D:2 * D], 1.0, None, ALU.add)
                v.tensor_scalar(g2_bf[0:1, :], mod_row[0:1, 4 * D:5 * D], 1.0, None, ALU.add)
                v.tensor_copy(sa_bf[0:1, :], mod_row[0:1, 0:D])
                v.tensor_copy(sbv_bf[0:1, :], mod_row[0:1, 3 * D:4 * D])
                for off in (2 * D, 5 * D):
                    g = mod_row[0:1, off:off + D]
                    sgn = msc_row[0:1, 0:D]
                    mx = msc_row[0:1, D:2 * D]
                    v.tensor_single_scalar(sgn, g, 0.0, ALU.is_ge)
                    v.drain()
                    v.tensor_scalar(sgn, sgn, 2.0, -1.0, ALU.mult, ALU.add)
                    v.drain()
                    v.tensor_mul(mx, g, sgn)
                    v.drain()
                    v.tensor_scalar(mx, mx, 0.25, None, ALU.max)
                    v.drain()
                    inst = v.tensor_mul(g, sgn, mx)
                    v.drain()
                return inst
            E["modrows"] = P.emit("v", modrows, sem="v")

            P.wait("sy", E["modrows"])
            e1 = P.dma("sy", gcd[0:1, :], mod_row[0:1, 2 * D:3 * D])
            e2 = P.dma("sy", gcd[1:2, :], mod_row[0:1, 5 * D:6 * D])
            P.waits("sy", e1, e2)
            P.dma("sy", ga_col[:, :],
                  gcd[0].rearrange("(c p) -> p c", p=128),
                  allow_slow_non_contiguous=True)
            E["gcols"] = P.dma(
                "sy", gb_col[:, :],
                gcd[1].rearrange("(c p) -> p c", p=128),
                allow_slow_non_contiguous=True)
            if debug:
                P.dma("sy", dbg["mod"], mod_row[0:1, :])

        ln_phase(P, E, dict(
            mid=mod_mid,
            tag="ln1", B=B, x_fm=x_fm, xbf_t=xbf_t, xsq_t=xsq_t, h1_t=h1_t,
            t_f32=t_f32, rows_f=rows_f, rows_b=rows_b, ones_col=ones_col,
            mone_row=mone_row, g_bf=g1_bf, s_bf=sa_bf,
            pre=("x",), bpre=("modrows",),
            cons=dict(kind="qku", wq_s=wq_s, wk_s=wk_s, wv1_s=wv1_s,
                      q_loc=q_loc, k_loc=k_loc, u_fm_s=u_fm_s,
                      u_tm_loc=u_tm_loc, ident=ident),
        ))

        P.waits("sy", E["ln1_done"])
        NK = KD * TCORE
        E["kst"] = P.dma("sy", cku_in[0:NK].rearrange("(a b) -> a b", b=TCORE),
                         k_loc[:, :])
        E["ust"] = P.dma("sy", cku_in[NK:NKU].rearrange("(a b) -> a b", b=1024),
                         u_tm_loc.rearrange("p a b -> p (a b)"))
        P.waits("g", E["kst"], E["ust"])
        E["dummy2st"] = P.dma("g", ccd2_in, warm[0:1, 0:8])
        P.wait("g", E["dummy2st"])
        E["kuag"] = P.emit("g", lambda g: g.collective_compute(
            "AllGather", ALU.bypass, replica_groups=rg,
            ins=[cku_in], outs=[cku_out]), sem="g")
        E["dummy2"] = P.emit("g", lambda g: g.collective_compute(
            "AllGather", ALU.bypass, replica_groups=rg,
            ins=[ccd2_in], outs=[ccd2_out]), sem="g")
        P.wait("sy", E["dummy2"])
        for c in range(NCH):
            rk = min(c, 15 - c)
            hf = 0 if c < 8 else 1
            kblk = cku_out[rk, 0:NK].rearrange("(a b) -> a b", b=TCORE)
            ublk = cku_out[rk, NK:NKU].rearrange("(a b) -> a b", b=1024)
            P.dma("sy", k_gath[:, c, :],
                  kblk[:, 1024 * hf:1024 * (hf + 1)])
            ev = P.dma("sy", u_gath[:, c, :],
                       ublk[:, 512 * hf:512 * (hf + 1)])
        E["kuload"] = ev

        if debug:
            P.wait("sy", E["kuload"])
            P.wait("g", E["kuload"])
            P.dma("g", dbg["q"], q_loc[:, :])
            P.dma("g", dbg["k"], k_gath.rearrange("k a b -> k (a b)"))
            P.dma("g", dbg["u"], u_gath.rearrange("k a b -> k (a b)"))

        if upto_phase >= 2:
            attention_phase(P, E, dict(
                B=B, x_fm=x_fm, q_loc=q_loc, k_loc=k_loc, u_tm_loc=u_tm_loc,
                k_gath=k_gath, u_gath=u_gath, p_t=p_t, p2_t=p2_t,
                B01=B01, B56=B56, uslot=uslot, o64acc=o64acc,
                ks_slot=ks_slot, qs_slot=qs_slot, jtab_s=jtab_s,
                wtab_s=wtab_s, wtab_b=wtab_b,
                uslot_d=uslot_d, masks=masks, btab_s=btab_s, att_f=att_f,
                att_b=att_b, rec_p0=rec_p0, rb_s=rb_s, o64n=o64n,
                ones_row=ones_row, wv2_s=wv2_s, ga_col=ga_col))
            if debug:
                P.wait("sy", E["attn_done"])
                P.dma("sy", dbg["x1"], x_fm[:, :, :])

        if upto_phase >= 3:
            P.barrier_all()
            ln_phase(P, E, dict(
                tag="ln2", B=B, x_fm=x_fm, xbf_t=xbf_t, xsq_t=xsq_t, h1_t=h1_t,
                t_f32=t_f32, rows_f=rows_f, rows_b=rows_b, ones_col=ones_col,
                mone_row=mone_row, g_bf=g2_bf, s_bf=sbv_bf,
                pre=("attn_done",), bpre=("modrows",),
                cons=dict(kind="h2", h2=h2),
            ))
            if debug:
                P.wait("sy", E["ln2_done"])
                P.wait("g", E["ln2_done"])
                P.dma("g", dbg["h2"], h2)

        if upto_phase >= 4:
            routing_phase(P, E, dict(
                B=B, h2=h2, routw_s=routw_s, eb_s=eb_s, rt_s=rt_s,
                act_tm=act_tm, cf_tm=cf_tm, tcols=tcols, cf_rows=cf_rows,
                ident=ident))
            if debug:
                P.wait("sy", E["cf_done"])
                P.wait("g", E["cf_done"])
                P.dma("g", dbg["cf"], cf_rows[:, :])

        if upto_phase >= 5:
            ffn_phase(P, E, dict(
                B=B, x_fm=x_fm, h2=h2, hdn=hdn, cfb_t=cfb_t, cf_rows=cf_rows,
                w1s=w1s, w2s=w2s, b1s=b1s, b2_s=b2_s, gb_col=gb_col,
                ones_row=ones_row, sel9=sel9, sw1_in=sw1_in, sb1_in=sb1_in,
                sw2_in=sw2_in, b2_in=b2_in, rw1_in=rw1_in, rb1_in=rb1_in,
                rw2_in=rw2_in))

        final_ev = {1: "ln1_done", 2: "attn_done", 3: "ln2_done",
                    4: "cf_done", 5: "ffn_done"}[min(upto_phase, 5)]
        P.wait("sy", E[final_ev])
        P.dma("sy", y_out, x_fm[:, :, :])
        if enddump:
            ed = {}
            for nm, shape in [("ga", [128, DC]), ("gb", [128, DC]),
                              ("qv", [KD, TCORE]), ("cfr", [9, TCORE]),
                              ("h2v", [128, DC * TCORE]), ("modv", [1, 6 * D])]:
                ed[nm] = dt("ed_" + nm, shape, F32, kind="ExternalOutput").ap()
            P.wait("g", E[final_ev])
            P.dma("g", ed["ga"], ga_col[:, :])
            P.dma("g", ed["gb"], gb_col[:, :])
            P.dma("g", ed["qv"], q_loc[:, :])
            P.dma("g", ed["cfr"], cf_rows[:, :])
            P.dma("g", ed["h2v"], h2.rearrange("p a b -> p (a b)"))
            P.dma("g", ed["modv"], mod_row[0:1, :])
        P.final_wait_all_dma()

        P.flush(blk)

    nc.compile()
    return nc


def ln_phase(P, E, A):
    tag, B, x_fm = A["tag"], A["B"], A["x_fm"]
    cons = A["cons"]
    rows_f, rows_b = A["rows_f"], A["rows_b"]
    g_bf, s_bf = A["g_bf"], A["s_bf"]
    pre = [E[k] for k in A["pre"]]
    bpre_keys = A["bpre"]

    for pan in range(NPAN):
        sl = slice(PW * pan, PW * (pan + 1))
        pb = pan % 2
        for c in range(DC):
            j = pan * DC + c
            i3 = j % 3
            P.waits("v", *pre)
            P.waits("s", *pre)
            if j >= 3:
                P.wait("v", E[(tag, "statmm", j - 3)])
                P.wait("s", E[(tag, "statmm", j - 3)])
            E[(tag, "xbf", j)] = P.emit("v", lambda v, c=c, i3=i3, sl=sl: v.tensor_copy(
                A["xbf_t"][:, i3, :], x_fm[:, c, sl]), sem="v")
            E[(tag, "xsq", j)] = P.emit("s", lambda s, c=c, i3=i3, sl=sl: s.activation(
                A["xsq_t"][:, i3, :], x_fm[:, c, sl], AF.Square), sem="s")
            P.waits("t", E[(tag, "xbf", j)], E[(tag, "xsq", j)])
            if c == 0 and pan >= 1:
                P.wait("t", E[(tag, "statrd", pan - 1)])
                P.wait("t", E[(tag, "hB", pan * DC - 1)])
            P.emit("t", lambda t, c=c, i3=i3: t.matmul(
                B[3][0:1, :], A["ones_col"][:, :], A["xbf_t"][:, i3, :],
                start=(c == 0), stop=(c == DC - 1), skip_group_check=True))
            E[(tag, "statmm", j)] = P.emit("t", lambda t, c=c, i3=i3: t.matmul(
                B[7][0:1, :], A["ones_col"][:, :], A["xsq_t"][:, i3, :],
                start=(c == 0), stop=(c == DC - 1), skip_group_check=True),
                sem="t")

        P.wait("v", E[(tag, "statmm", pan * DC + DC - 1)])
        if pan >= 2:
            P.wait("v", E[(tag, "bc", (pan - 1) * DC - 1)])
        mu = rows_f[0:1, 0, :]
        e2 = rows_f[0:1, 1, :]
        sd = rows_f[0:1, 2, :]
        mur = rows_f[0:1, 3, :]
        r_bf = rows_b[0:1, 2 * pb + 0, :]
        mur_bf = rows_b[0:1, 2 * pb + 1, :]
        if pan >= 1:
            P.wait("v", E[(tag, "rows", pan - 1)])

        def rows1(v, mu=mu, e2=e2):
            v.tensor_scalar(mu, B[3][0:1, :], 1.0 / D, None, ALU.mult)
            v.tensor_scalar(e2, B[7][0:1, :], 1.0 / D, None, ALU.mult)
            return v.drain()
        E[(tag, "statrd", pan)] = P.emit("v", rows1, sem="v")

        def rows2(v, mu=mu, e2=e2, sd=sd):
            v.tensor_mul(sd, mu, mu)
            v.drain()
            v.tensor_sub(e2, e2, sd)
            v.drain()
            v.tensor_scalar(e2, e2, 1e-5, None, ALU.add)
            return v.drain()
        E[(tag, "var", pan)] = P.emit("v", rows2, sem="v")
        P.wait("s", E[(tag, "var", pan)])
        E[(tag, "sd", pan)] = P.emit("s", lambda s, sd=sd, e2=e2: s.activation(
            sd, e2, AF.Sqrt), sem="s")
        P.wait("v", E[(tag, "sd", pan)])

        def rows3(v, sd=sd, mu=mu, mur=mur, r_bf=r_bf, mur_bf=mur_bf):
            v.reciprocal(sd, sd)
            v.drain()
            v.tensor_copy(r_bf, sd)
            v.tensor_mul(mur, mu, sd)
            v.drain()
            v.tensor_copy(mur_bf, mur)
            return v.drain()
        E[(tag, "rows", pan)] = P.emit("v", rows3, sem="v")

        if pan == 0 and A.get("mid") is not None:
            A["mid"]()

        P.wait("t", E[(tag, "rows", pan)])
        for k in bpre_keys:
            P.wait("t", E[k])
        LAG2 = 2

        def emit_bc(c, pan=pan, sl=sl, r_bf=r_bf, mur_bf=mur_bf):
            j = pan * DC + c
            rb = 4 if c % 2 == 0 else 6
            mb = 5 if c % 2 == 0 else 7
            if j >= 2:
                P.wait("t", E[(tag, "hA", j - 2)])
                P.wait("t", E[(tag, "hB", j - 2)])
            P.emit("t", lambda t, c=c, rb=rb: t.matmul(
                B[rb][:, :], g_bf[0:1, 128 * c:128 * (c + 1)], r_bf,
                start=True, stop=True, skip_group_check=True))
            P.emit("t", lambda t, c=c, mb=mb: t.matmul(
                B[mb][:, :], g_bf[0:1, 128 * c:128 * (c + 1)], mur_bf,
                start=True, stop=False, skip_group_check=True))
            E[(tag, "bc", j)] = P.emit("t", lambda t, c=c, mb=mb: t.matmul(
                B[mb][:, :], s_bf[0:1, 128 * c:128 * (c + 1)], A["mone_row"][0:1, :],
                start=False, stop=True, skip_group_check=True), sem="t")

            P.wait("v", E[(tag, "bc", j)])
            if cons["kind"] == "qku" and j >= 3:
                P.wait("v", E[(tag, "hfree", j - 3)])
            tf = A["t_f32"][:, j % 2, :]
            rb2 = 4 if c % 2 == 0 else 6
            E[(tag, "hA", j)] = P.emit("v", lambda v, tf=tf, c=c, sl=sl, rb2=rb2:
                                       v.tensor_mul(tf, x_fm[:, c, sl], B[rb2][:, :]),
                                       sem="v")
            P.wait("v", E[(tag, "hA", j)])
            if cons["kind"] == "qku":
                hdst = A["h1_t"][:, j % 3, :]
            else:
                hdst = cons["h2"][:, c, sl]
            mb2 = 5 if c % 2 == 0 else 7
            E[(tag, "hB", j)] = P.emit("v", lambda v, hdst=hdst, tf=tf, mb2=mb2:
                                       v.tensor_sub(hdst, tf, B[mb2][:, :]), sem="v")

        def emit_qku(c, pan=pan):
            j = pan * DC + c
            i3 = j % 3
            P.wait("t", E[(tag, "hB", j)])
            if c == 0:
                P.waits("t", E["wq"], E["wk"], E["wv1"])
                if pan >= 1:
                    P.wait("t", E[(tag, "utmc", (pan - 1) * 4 + 3)])
                    P.wait("t", E[(tag, "qkuev", pan - 1)])
            h = A["h1_t"][:, i3, :]
            P.emit("t", lambda t, c=c, h=h: t.matmul(
                B[0][0:KD, :], cons["wq_s"][:, c, :], h,
                start=(c == 0), stop=(c == DC - 1), skip_group_check=True))
            P.emit("t", lambda t, c=c, h=h: t.matmul(
                B[1][0:KD, :], cons["wk_s"][:, c, :], h,
                start=(c == 0), stop=(c == DC - 1), skip_group_check=True))
            E[(tag, "hfree", j)] = P.emit("t", lambda t, c=c, h=h: t.matmul(
                B[2][0:KD, :], cons["wv1_s"][:, c, :], h,
                start=(c == 0), stop=(c == DC - 1), skip_group_check=True),
                sem="t")

        if cons["kind"] == "qku":
            for step in range(DC + LAG2):
                if step < DC:
                    emit_bc(step)
                if step >= LAG2:
                    emit_qku(step - LAG2)
        else:
            for c in range(DC):
                emit_bc(c)

        if cons["kind"] == "qku":
            P.wait("v", E[(tag, "hfree", pan * DC + DC - 1)])

            def qkuev(v, sl=sl, pan=pan):
                v.tensor_scalar(cons["q_loc"][:, pan, :], B[0][0:KD, :], 0.125,
                                None, ALU.mult)
                v.tensor_copy(cons["k_loc"][:, sl], B[1][0:KD, :])
                v.tensor_copy(cons["u_fm_s"][:, :], B[2][0:KD, :])
                return v.drain()
            E[(tag, "qkuev", pan)] = P.emit("v", qkuev, sem="v")

            for tt in range(4):
                i = pan * 4 + tt
                P.wait("t", E[(tag, "qkuev", pan)])
                if i >= 1:
                    P.wait("t", E[(tag, "utmc", i - 1)])
                E[(tag, "utp", i)] = P.emit("t", lambda t, tt=tt: t.transpose(
                    B[2][:, 0:KD], cons["u_fm_s"][:, 128 * tt:128 * (tt + 1)],
                    cons["ident"][0:KD, 0:KD]), sem="t")
                P.wait("v", E[(tag, "utp", i)])
                E[(tag, "utmc", i)] = P.emit("v", lambda v, i=i: v.tensor_copy(
                    cons["u_tm_loc"][:, i, :], B[2][:, 0:KD]), sem="v")

    if cons["kind"] == "qku":
        E["ln1_done"] = E[(tag, "utmc", NPAN * 4 - 1)]
    else:
        E["ln2_done"] = E[(tag, "hB", NPAN * DC - 1)]


def attention_phase(P, E, A):
    B, x_fm = A["B"], A["x_fm"]
    B01, B56 = A["B01"], A["B56"]

    P.wait("g", E["modsl_ps"])

    def mk_masks0(g):
        for k in range(4):
            g.memset(A["masks"][:, k, :], 1.0)
        for b in range(2):
            for tt in range(8):
                g.memset(A["uslot"][:, b, 65 * tt + 64:65 * tt + 65], 1.0)
        for tt in range(8):
            g.memset(A["uslot_d"][:, 65 * tt + 64:65 * tt + 65], 1.0)
        return g.drain()
    E["maskinit0"] = P.emit("g", mk_masks0, sem="g")
    P.wait("g", E["maskinit0"])

    def mk_masks(g):
        for k in range(4):
            g.affine_select(
                out=A["masks"][:, k, :], in_=A["masks"][:, k, :],
                compare_op=ALU.is_ge, fill=0.0, base=-128 * k,
                pattern=[[1, PW]], channel_multiplier=-1)
        return g.drain()
    E["maskinit"] = P.emit("g", mk_masks, sem="g")
    P.wait("g", E["ln1_done"])
    E["accz"] = P.emit("g", lambda g: g.memset(A["o64acc"][0:65, :, :], 0.0),
                       sem="g")

    NJOBS = 17
    units = []
    jobpan = {}
    for j in range(2):
        for pan, tts in ((0, [(tt, True) for tt in range(4)]),
                         (1, [(tt, tt >= 4) for tt in range(8)])):
            for tt, mk in tts:
                jobpan.setdefault((j, pan), []).append(len(units))
                units.append((j, "diag", j, pan, tt, mk))
    for j in range(2, NJOBS):
        for pan in range(2):
            for tt in range(8):
                jobpan.setdefault((j, pan), []).append(len(units))
                units.append((j, "full", None, pan, tt, False))
    nu = len(units)
    pan_seq = []
    for j in range(NJOBS):
        for pan in range(2):
            pan_seq.append((j, pan))
    pan_idx = {jp: i for i, jp in enumerate(pan_seq)}

    STB = (0, 1, 5, 6)
    LAG = 3
    qreg = {}

    def stage_job(j):
        P.waits("v", E["kuload"], E["jtab"])
        if j >= 4:
            lastu = jobpan[(j - 2, 1)][-1]
            P.wait("v", E[("avmm", lastu)])

        def kstage(v, j=j):
            cidx = v.value_load(A["jtab_s"][0:1, 4 * (j - 2):4 * (j - 2) + 1],
                                min_val=0, max_val=NCH - 1)
            qq = v.value_load(A["jtab_s"][0:1, 4 * (j - 2) + 2:4 * (j - 2) + 3],
                              min_val=0, max_val=2)
            v.tensor_copy(A["ks_slot"][0:64, j % 2, :],
                          A["k_gath"][0:64, cidx, :])
            for pan2 in range(2):
                v.tensor_copy(A["qs_slot"][0:64, 2 * (j % 2) + pan2, :],
                              A["q_loc"][:, qq + pan2, :])
            for t8 in range(8):
                inst = v.tensor_copy(
                    A["uslot"][:, j % 2, 65 * t8:65 * t8 + 64],
                    A["u_gath"][:, cidx, 64 * t8:64 * (t8 + 1)])
            return inst
        E[("kstage", j)] = P.emit("v", kstage, sem="v")

    def emit_S(ui):
        j, kindj, qts, pan, tt, masked = units[ui]
        stb = STB[ui % 4]
        if kindj == "diag" and qts == 1 and pan == 1 and tt == 0:
            stage_job(2)
        if kindj == "full" and pan == 0 and tt == 4 and j + 1 < NJOBS:
            stage_job(j + 1)
        if ui >= 4:
            P.wait("t", E[("expp", (ui - 4) // 2)])
        if kindj == "full":
            P.wait("t", E[("kstage", j)])
            klhs = A["ks_slot"][0:64, j % 2, 128 * tt:128 * (tt + 1)]
            qrhs = A["qs_slot"][0:64, 2 * (j % 2) + pan, :]
        else:
            P.wait("t", E["ln1_done"])
            klhs = A["k_loc"][:, 1024 * qts + 128 * tt:1024 * qts + 128 * (tt + 1)]
            qrhs = A["q_loc"][:, qts * 2 + pan, :]
        E[("smm", ui)] = P.emit("t", lambda t, klhs=klhs, stb=stb, qrhs=qrhs:
                                t.matmul(B[stb][:, :], klhs, qrhs,
                                         start=True, stop=True,
                                         skip_group_check=True), sem="t")

        if ui % 2 == 1:
            pi = ui // 2
            P.wait("s", E[("smm", ui)])
            if ui >= 4:
                P.wait("s", E[("avmm", ui - 4)])
            src_ps = B01 if pi % 2 == 0 else B56
            E[("expp", pi)] = P.emit(
                "s", lambda s, pi=pi, src_ps=src_ps:
                s.activation(A["p2_t"][:, pi % 2, :], src_ps[:, :], AF.Exp),
                sem="s")
            for uu in (ui - 1, ui):
                ju, kj2, q2, pan2, tt2, mk2 = units[uu]
                if not mk2:
                    continue
                P.waits("v", E[("expp", pi)], E["maskinit"])
                mkt = tt2 if pan2 == 0 else tt2 - 4
                E[("pmask", uu)] = P.emit(
                    "v", lambda v, pb4=uu % 4, mkt=mkt: v.tensor_mul(
                        A["p_t"][:, pb4, :], A["p_t"][:, pb4, :],
                        A["masks"][:, mkt, :]), sem="v")

    def emit_AV(ui):
        j, kindj, qts, pan, tt, masked = units[ui]
        pb2 = ui % 4
        ju = jobpan[(j, pan)]
        first = (ui == ju[0])
        last = (ui == ju[-1])
        ob = 2 + (pan_idx[(j, pan)] % 2)
        if masked:
            P.wait("t", E[("pmask", ui)])
        else:
            P.wait("t", E[("expp", ui // 2)])
        if kindj == "full":
            ulhs = A["uslot"][:, j % 2, 65 * tt:65 * (tt + 1)]
        else:
            P.wait("t", E[("udiag", j)])
            ulhs = A["uslot_d"][:, 65 * tt:65 * (tt + 1)]
        if first and pan_idx[(j, pan)] >= 2:
            P.wait("t", E[("accadd", pan_idx[(j, pan)] - 2)])
        E[("avmm", ui)] = P.emit(
            "t", lambda t, ulhs=ulhs, pb2=pb2, ob=ob, first=first, last=last:
            t.matmul(B[ob][0:65, :], ulhs, A["p_t"][:, pb2, :],
                     start=first, stop=last, skip_group_check=True), sem="t")
        if last:
            pidx = pan_idx[(j, pan)]
            P.waits("v", E[("avmm", ui)], E["accz"], E["wtab"], E["wtbb"])

            def accadd(v, j=j, kindj=kindj, qts=qts, pan=pan, ob=ob):
                if kindj == "diag":
                    dst = A["o64acc"][0:65, qts * 2 + pan, :]
                    return v.tensor_add(dst, dst, B[ob][0:65, :])
                dA = A["o64acc"][0:65, pan, :]
                dB = A["o64acc"][0:65, 2 + pan, :]
                v.scalar_tensor_tensor(
                    out=dA, in0=B[ob][0:65, :], scalar=A["wtab_s"][0:65, j - 2:j - 1],
                    in1=dA, op0=ALU.mult, op1=ALU.add)
                return v.scalar_tensor_tensor(
                    out=dB, in0=B[ob][0:65, :],
                    scalar=A["wtab_b"][0:65, j - 2:j - 1],
                    in1=dB, op0=ALU.mult, op1=ALU.add)
            E[("accadd", pidx)] = P.emit("v", accadd, sem="v")

    for qt in range(2):
        P.waits("v", E["maskinit"], E["ln1_done"])
        if qt == 1:
            P.wait("v", E[("avmm", jobpan[(0, 1)][-1])])

        def udiag(v, qt=qt):
            for tt in range(8):
                inst = v.tensor_copy(A["uslot_d"][:, 65 * tt:65 * tt + 64],
                                     A["u_tm_loc"][:, qt * 8 + tt, :])
            return inst
        E[("udiag", qt)] = P.emit("v", udiag, sem="v")
        if qt == 0:
            for step in range(len(units)):
                if step < nu:
                    emit_S(step)
                if step >= LAG:
                    emit_AV(step - LAG)
                if qt == 0 and step + 1 == jobpan[(1, 0)][0] + LAG:
                    break
    start_step = jobpan[(1, 0)][0] + LAG
    for step in range(start_step, nu + LAG):
        if step < nu:
            emit_S(step)
        if step >= LAG:
            emit_AV(step - LAG)
    E["accall"] = E[("accadd", len(pan_seq) - 1)]

    for seq in range(4):
        qt, pan = seq // 2, seq % 2
        gcol = qt * 1024 + pan * PW
        acc = A["o64acc"]
        P.waits("v", E["accall"])
        if seq >= 1:
            P.wait("v", E[("recp0", seq - 1)])

        def rec(v, seq=seq):
            v.reciprocal(A["att_f"][64:65, :], acc[64:65, seq, :])
            v.drain()
            v.tensor_copy(A["att_b"][64:65, :], A["att_f"][64:65, :])
            return v.drain()
        E[("rec", seq)] = P.emit("v", rec, sem="v")

        P.wait("sy", E[("rec", seq)])
        if seq >= 1:
            P.wait("sy", E[("recb", seq - 1)])
        E[("recp0", seq)] = P.dma("sy", A["rec_p0"][0:1, :], A["att_b"][64:65, :])

        P.wait("t", E[("recp0", seq)])
        if seq >= 1:
            P.wait("t", E[("o64n", seq - 1)])
            P.wait("t", E[("xupd", seq - 1, 6)])
        E[("recb", seq)] = P.emit("t", lambda t: t.matmul(
            B[4][0:KD, :], A["ones_row"][0:1, 0:KD], A["rec_p0"][0:1, :],
            start=True, stop=True, skip_group_check=True), sem="t")

        P.wait("v", E[("recb", seq)])
        if seq >= 1:
            P.wait("v", E[("v2", seq - 1, DC - 1)])

        def o64nf(v, seq=seq):
            v.tensor_copy(A["rb_s"][:, :], B[4][0:KD, :])
            v.drain()
            v.tensor_mul(A["o64n"][:, :], acc[0:64, seq, :], A["rb_s"][:, :])
            return v.drain()
        E[("o64n", seq)] = P.emit("v", o64nf, sem="v")

        P.waits("t", E[("o64n", seq)], E["wv2"])
        for c in range(DC):
            ab = 4 if c % 2 == 0 else 7
            if c >= 2:
                P.wait("t", E[("xupd", seq, c - 2)])
            elif seq >= 1:
                P.wait("t", E[("xupd", seq - 1, DC - 2 + c)])
            E[("v2", seq, c)] = P.emit("t", lambda t, ab=ab, c=c: t.matmul(
                B[ab][:, :], A["wv2_s"][:, 128 * c:128 * (c + 1)],
                A["o64n"][:, :], start=True, stop=True,
                skip_group_check=True), sem="t")
            P.wait("v", E[("v2", seq, c)])
            P.wait("v", E["gcols"])
            xs = x_fm[:, c, gcol:gcol + PW]
            E[("xupd", seq, c)] = P.emit("v", lambda v, ab=ab, c=c, xs=xs:
                                         v.scalar_tensor_tensor(
                                             out=xs, in0=B[ab][:, :],
                                             scalar=A["ga_col"][:, c:c + 1],
                                             in1=xs, op0=ALU.mult, op1=ALU.add),
                                         sem="v")

    E["attn_done"] = E[("xupd", 3, DC - 1)]


def routing_phase(P, E, A):
    B = A["B"]
    for pan in range(NPAN):
        lb = pan % 2
        sl = slice(PW * pan, PW * (pan + 1))
        P.waits("t", E["ln2_done"], E["routw"])
        if pan >= 2:
            P.wait("t", E[("sp", pan - 2)])
        for c in range(DC):
            ev = P.emit("t", lambda t, c=c, lb=lb, sl=sl: t.matmul(
                B[lb][0:8, :], A["routw_s"][:, c, :], A["h2"][:, c, sl],
                start=(c == 0), stop=(c == DC - 1), skip_group_check=True),
                sem="t" if c == DC - 1 else None)
        E[("logit", pan)] = ev

        P.waits("s", E[("logit", pan)], E["eb"])
        E[("spe", pan)] = P.emit("s", lambda s, lb=lb, sl=sl: s.activation(
            A["rt_s"][0:8, sl], B[lb][0:8, :], AF.Exp, bias=A["eb_s"][:, :]),
            sem="s")
        P.wait("v", E[("spe", pan)])
        E[("sp1", pan)] = P.emit("v", lambda v, sl=sl: v.tensor_scalar(
            A["rt_s"][0:8, sl], A["rt_s"][0:8, sl], 1.0, None, ALU.add), sem="v")
        P.wait("s", E[("sp1", pan)])

        def sp(s, sl=sl):
            s.activation(A["rt_s"][0:8, sl], A["rt_s"][0:8, sl], AF.Ln)
            s.drain()
            s.activation(A["rt_s"][0:8, sl], A["rt_s"][0:8, sl], AF.Sqrt)
            return s.drain()
        E[("sp", pan)] = P.emit("s", sp, sem="s")

        for tt in range(4):
            i = pan * 4 + tt
            P.wait("t", E[("sp", pan)])
            if i >= 1:
                P.wait("t", E[("acttmc", i - 1)])
            E[("acttp", i)] = P.emit("t", lambda t, pan=pan, tt=tt: t.transpose(
                B[2][:, 0:8],
                A["rt_s"][0:8, PW * pan + 128 * tt:PW * pan + 128 * (tt + 1)],
                A["ident"][0:8, 0:8]), sem="t")
            P.wait("v", E[("acttp", i)])
            E[("acttmc", i)] = P.emit("v", lambda v, i=i: v.tensor_copy(
                A["act_tm"][:, i, :], B[2][:, 0:8]), sem="v")

            P.wait("v", E[("acttmc", i)])

            def top2(v, i=i):
                a = A["act_tm"][:, i, :]
                scr = A["cf_tm"][:, i, :]
                m1 = A["tcols"][:, 0:1]
                m2 = A["tcols"][:, 1:2]
                v.reduce_max(m1, a, axis=AX)
                v.drain()
                v.scalar_tensor_tensor(out=scr, in0=a, scalar=m1, in1=a,
                                       op0=ALU.subtract, op1=ALU.bypass)
                v.drain()
                v.tensor_single_scalar(scr, scr, 0.0, ALU.is_equal)
                v.drain()
                v.tensor_single_scalar(scr, scr, NEG, ALU.mult)
                v.drain()
                v.tensor_add(scr, scr, a)
                v.drain()
                v.reduce_max(m2, scr, axis=AX)
                v.drain()
                v.scalar_tensor_tensor(out=scr, in0=a, scalar=m2, in1=a,
                                       op0=ALU.subtract, op1=ALU.bypass)
                v.drain()
                v.tensor_single_scalar(scr, scr, 0.0, ALU.is_ge)
                v.drain()
                v.tensor_mul(scr, scr, a)
                return v.drain()
            E[("cf_tm", i)] = P.emit("v", top2, sem="v")

            P.wait("t", E[("cf_tm", i)])
            if pan >= 1 and tt == 0:
                P.wait("t", E[("cfrow", pan - 1)])
            E[("cftp", i)] = P.emit("t", lambda t, tt=tt, i=i: t.transpose(
                B[3][0:8, 128 * tt:128 * (tt + 1)], A["cf_tm"][:, i, :],
                A["ident"][:, :]), sem="t")

        P.wait("v", E[("cftp", pan * 4 + 3)])
        if pan == 0:
            P.wait("g", E["ln2_done"])
            E["cfones"] = P.emit("g", lambda g: g.memset(A["cf_rows"][0:9, :], 1.0),
                                 sem="g")
            P.wait("v", E["cfones"])
        E[("cfrow", pan)] = P.emit("v", lambda v, sl=sl: v.tensor_copy(
            A["cf_rows"][0:8, sl], B[3][0:8, :]), sem="v")

    E["cf_done"] = E[("cfrow", NPAN - 1)]


def ffn_phase(P, E, A):
    B, x_fm, h2, hdn = A["B"], A["x_fm"], A["h2"], A["hdn"]

    def load_member(m):
        s = m % 2
        if m >= 2:
            P.wait("g", E[("w2last", m - 2)])
        else:
            P.wait("g", E["attn_done"])
        if m == 0:
            P.dma("g", A["w1s"][s], A["sw1_in"])
            P.dma("g", A["b1s"][s], A["sb1_in"])
            E[("wl", m)] = P.dma("g", A["w2s"][s], A["sw2_in"])
            E["b2l"] = P.dma("g", A["b2_s"].rearrange("p (c f) -> p c f", f=128),
                             A["b2_in"])
        else:
            e = m - 1
            P.dma("g", A["w1s"][s], A["rw1_in"][e])
            P.dma("g", A["b1s"][s], A["rb1_in"][e])
            E[("wl", m)] = P.dma("g", A["w2s"][s], A["rw2_in"][e])

    load_member(0)
    load_member(1)

    for m in range(9):
        s = m % 2
        if m >= 1:
            for pan in range(NPAN):
                cb = 4 + pan % 2
                P.waits("t", E["cf_done"], E["cfones"])
                prev = ("cfbev", m, pan - 2) if pan >= 2 else \
                       (("cfbev", m - 1, pan + 2) if m >= 2 else None)
                if prev and prev in E:
                    P.wait("t", E[prev])
                if m == 1 and pan < 2:
                    P.wait("t", E["attn_done"])
                E[("cfb", m, pan)] = P.emit("t", lambda t, cb=cb, m=m, pan=pan:
                                            t.matmul(B[cb][:, :],
                                                     A["sel9"][:, m - 1, :],
                                                     A["cf_rows"][0:9,
                                                                  PW * pan:PW * (pan + 1)],
                                                     start=True, stop=True,
                                                     skip_group_check=True), sem="t")
                P.wait("v", E[("cfb", m, pan)])
                if m >= 2:
                    P.wait("v", E[("cfm", m - 1, pan * 4 + 3)])
                E[("cfbev", m, pan)] = P.emit("v", lambda v, cb=cb, pan=pan:
                                              v.tensor_copy(A["cfb_t"][:, pan, :],
                                                            B[cb][:, :]), sem="v")

        P.wait("t", E[("wl", m)])
        if m >= 1:
            P.wait("t", E[("w2last", m - 1)])
        for pan in range(NPAN):
            for hc in range(4):
                idx = pan * 4 + hc
                wb = idx % 2
                if idx >= 2:
                    P.wait("t", E[("silu", m, idx - 2)])
                elif m == 0:
                    P.wait("t", E[("sp", 2 + idx)])
                for c in range(DC):
                    ev = P.emit("t", lambda t, s=s, c=c, hc=hc, pan=pan, wb=wb:
                                t.matmul(B[wb][:, :],
                                         A["w1s"][s][:, c, 128 * hc:128 * (hc + 1)],
                                         h2[:, c, PW * pan:PW * (pan + 1)],
                                         start=(c == 0), stop=(c == DC - 1),
                                         skip_group_check=True),
                                sem="t" if c == DC - 1 else None)
                E[("w1", m, idx)] = ev

                P.wait("s", E[("w1", m, idx)])
                E[("silu", m, idx)] = P.emit(
                    "s", lambda sc, s=s, hc=hc, pan=pan, wb=wb: sc.activation(
                        hdn[:, hc, PW * pan:PW * (pan + 1)], B[wb][:, :], AF.Silu,
                        bias=A["b1s"][s][:, hc:hc + 1]), sem="s")

                if m >= 1:
                    P.waits("v", E[("silu", m, idx)], E[("cfbev", m, pan)])
                    E[("cfm", m, idx)] = P.emit(
                        "v", lambda v, hc=hc, pan=pan: v.tensor_mul(
                            hdn[:, hc, PW * pan:PW * (pan + 1)],
                            hdn[:, hc, PW * pan:PW * (pan + 1)],
                            A["cfb_t"][:, pan, :]), sem="v")

        if m == 8:
            P.waits("t", E["b2l"], E["cfones"], E["cf_done"])
        for pan in range(NPAN):
            for hc in range(4):
                idx = pan * 4 + hc
                key = ("cfm", m, idx) if m >= 1 else ("silu", m, idx)
                P.wait("t", E[key])
        for pan in range(NPAN):
            for c in range(DC):
                idx = pan * DC + c
                rbk = 2 + idx % 2
                if idx >= 2:
                    P.wait("t", E[("xffn", m, idx - 2)])
                elif m == 0:
                    P.wait("t", E[("acttmc", 15)])
                    P.wait("t", E[("cfrow", 3)])
                else:
                    P.wait("t", E[("xffn", m - 1, NPAN * DC - 2 + idx)])
                for hc in range(4):
                    ev = P.emit("t", lambda t, s=s, hc=hc, c=c, pan=pan, rbk=rbk,
                                m=m: t.matmul(
                                    B[rbk][:, :],
                                    A["w2s"][s][:, hc, 128 * c:128 * (c + 1)],
                                    hdn[:, hc, PW * pan:PW * (pan + 1)],
                                    start=(hc == 0),
                                    stop=(hc == 3 and m != 8),
                                    skip_group_check=True),
                                sem="t" if (hc == 3 and m != 8) else None)
                if m == 8:
                    ev = P.emit("t", lambda t, c=c, pan=pan, rbk=rbk: t.matmul(
                        B[rbk][:, :], A["b2_s"][:, 128 * c:128 * (c + 1)],
                        A["cf_rows"][:, PW * pan:PW * (pan + 1)],
                        start=False, stop=True, skip_group_check=True), sem="t")
                E[("w2", m, idx)] = ev

                P.wait("v", E[("w2", m, idx)])
                P.wait("v", E["gcols"])
                xs = x_fm[:, c, PW * pan:PW * (pan + 1)]
                E[("xffn", m, idx)] = P.emit(
                    "v", lambda v, rbk=rbk, c=c, xs=xs: v.scalar_tensor_tensor(
                        out=xs, in0=B[rbk][:, :], scalar=A["gb_col"][:, c:c + 1],
                        in1=xs, op0=ALU.mult, op1=ALU.add), sem="v")
        E[("w2last", m)] = E[("w2", m, NPAN * DC - 1)]
        E[("w2done", m)] = E[("xffn", m, NPAN * DC - 1)]
        if m + 2 <= 8:
            load_member(m + 2)

    E["ffn_done"] = E[("xffn", 8, NPAN * DC - 1)]


_CACHE = {}


def _host_prep(inputs):
    x = np.asarray(inputs["L0_attn_x"], dtype=np.float32)[0]
    wq = np.asarray(inputs["L0_attn_w_q"], dtype=np.float32)
    wk = np.asarray(inputs["L0_attn_w_k"], dtype=np.float32)
    wv1 = np.asarray(inputs["L0_attn_w_v1"], dtype=np.float32)
    wv2 = np.asarray(inputs["L0_attn_w_v2"], dtype=np.float32)
    routw = np.asarray(inputs["L0_ffn_routing_W"], dtype=np.float32)
    eb = np.asarray(inputs["L0_ffn_expert_bias"], dtype=np.float32)
    sw1 = np.asarray(inputs["L0_ffn_sW1"], dtype=np.float32)
    sb1 = np.asarray(inputs["L0_ffn_sb1"], dtype=np.float32)
    sw2 = np.asarray(inputs["L0_ffn_sW2"], dtype=np.float32)
    sb2 = np.asarray(inputs["L0_ffn_sb2"], dtype=np.float32)
    rw1 = np.asarray(inputs["L0_ffn_rW1"], dtype=np.float32)
    rb1 = np.asarray(inputs["L0_ffn_rb1"], dtype=np.float32)
    rw2 = np.asarray(inputs["L0_ffn_rW2"], dtype=np.float32)
    rb2 = np.asarray(inputs["L0_ffn_rb2"], dtype=np.float32)
    cv = np.asarray(inputs["L0_c"], dtype=np.float32)
    wmod = np.asarray(inputs["L0_wmod"], dtype=np.float32)

    C = np.ascontiguousarray
    shared = dict(
        wq_in=C(wq.reshape(DC, 128, KD).transpose(1, 0, 2)),
        wk_in=C(wk.reshape(DC, 128, KD).transpose(1, 0, 2)),
        wv1_in=C(wv1.reshape(DC, 128, KD).transpose(1, 0, 2)),
        wv2_in=C(wv2),
        routw_in=C(routw.T.reshape(DC, 128, 8).transpose(1, 0, 2)),
        eb_in=C(eb.reshape(8, 1)),
        sw1_in=C(sw1.reshape(DC, 128, HD).transpose(1, 0, 2)),
        sb1_in=C(sb1.reshape(4, 128).T),
        sw2_in=C(sw2.reshape(4, 128, D).transpose(1, 0, 2)),
        b2_in=C(np.concatenate([rb2, sb2[None, :]], 0).reshape(9, DC, 128)),
        rw1_in=C(rw1.reshape(8, DC, 128, HD).transpose(0, 2, 1, 3)),
        rb1_in=C(rb1.reshape(8, 4, 128).transpose(0, 2, 1)),
        rw2_in=C(rw2.reshape(8, 4, 128, D).transpose(0, 2, 1, 3)),
        cvec_in=C(cv.reshape(1, D)),
    )
    per_core = []
    for i in range(NCORES):
        Ac, Bc = i, 15 - i
        toks = np.concatenate([x[1024 * Ac:1024 * (Ac + 1)],
                               x[1024 * Bc:1024 * (Bc + 1)]], axis=0)
        x_h = C(toks.T.reshape(DC, 128, TCORE).transpose(1, 0, 2))
        wmod_h = C(wmod[:, 768 * i:768 * (i + 1)].reshape(DC, 128, 768)
                   .transpose(1, 0, 2))
        btab = np.zeros((128, 2 * NCH), dtype=np.float32)
        for qt, ch in ((0, Ac), (1, Bc)):
            for c in range(NCH):
                btab[:, qt * NCH + c] = 0.0 if c < ch else NEG
        jobs = [(0, c) for c in range(Ac)] + [(1, c) for c in range(Bc)]
        assert len(jobs) == 15
        jtab = np.zeros((1, 60), dtype=np.int32)
        wta = np.zeros((65, 16), dtype=np.float32)
        wtb = np.zeros((65, 16), dtype=np.float32)
        for j, (qt, c) in enumerate(jobs):
            jtab[0, 4 * j + 0] = c
            jtab[0, 4 * j + 1] = c
            jtab[0, 4 * j + 2] = 2 * qt
            jtab[0, 4 * j + 3] = 2 * qt
            wta[:, j] = 1.0 if qt == 0 else 0.0
            wtb[:, j] = 1.0 if qt == 1 else 0.0
        d = dict(shared)
        d.update(x_in=x_h, wmod_in=wmod_h, btab_in=btab, jtab_in=jtab,
                 wtab_in=wta, wtbb_in=wtb)
        per_core.append(d)
    return per_core


def _reassemble(results):
    out = np.empty((1, T, D), dtype=np.float32)
    for i in range(NCORES):
        y = results[i]["y_out"]
        toks = np.ascontiguousarray(np.asarray(y).transpose(2, 1, 0)).reshape(TCORE, D)
        Ac, Bc = i, 15 - i
        out[0, 1024 * Ac:1024 * (Ac + 1)] = toks[:1024]
        out[0, 1024 * Bc:1024 * (Bc + 1)] = toks[1024:]
    return out


def kernel(**inputs):
    per_core = _host_prep(inputs)
    key = ("v1", DEBUG)
    if key not in _CACHE:
        _CACHE[key] = build_nc(upto_phase=5, debug=DEBUG)
    nc = _CACHE[key]
    res = run_bass_kernel_spmd(nc, per_core, core_ids=list(range(NCORES)))
    return _reassemble(res.results)


# version 11 (final)
# speedup vs baseline: 1.0351x; 1.0351x over previous
import numpy as np
from contextlib import ExitStack

import concourse.bass as bass
import concourse.mybir as mybir
from concourse.bass import ds
from concourse.bass_utils import run_bass_kernel_spmd
from concourse.bacc import Bacc
from concourse.replica_groups import maybe_share_collective_output_space

F32 = mybir.dt.float32
I32 = mybir.dt.int32
BF16 = mybir.dt.bfloat16
AF = mybir.ActivationFunctionType
ALU = mybir.AluOpType
AX = mybir.AxisListType.X

NCORES = 8
T = 16384
D = 1024
DC = 8
KD = 64
HD = 512
TCORE = 2048
PW = 512
NPAN = 4
NCH = 16
NEG = -1.0e9

DEBUG = False


class Prog:

    ENGS = ("sy", "g", "t", "s", "v")

    def __init__(self, nc, ctx):
        self.nc = nc
        self.q = {e: [] for e in self.ENGS}
        self.sems = {}
        for name in ("dx", "dw", "t", "s", "v", "g"):
            self.sems[name] = ctx.enter_context(nc.semaphore("sem_" + name))
        self.counts = {k: 0 for k in self.sems}
        self.waited = {}

    def emit(self, eng, fn, sem=None, amt=1):
        ev = None
        if sem is not None:
            self.counts[sem] += amt
            val = self.counts[sem]
            ev = (sem, val, eng)
            s = self.sems[sem]

            def wrapped(e, fn=fn, s=s, amt=amt):
                inst = fn(e)
                inst.then_inc(s, amt)
            self.q[eng].append(wrapped)
        else:
            self.q[eng].append(fn)
        return ev

    def dma(self, eng, out, in_, **kw):
        sem = "dw" if eng == "g" else "dx"
        return self.emit(eng, lambda e: e.dma_start(out=out, in_=in_, **kw),
                         sem=sem, amt=16)

    def wait(self, eng, ev):
        if ev is None:
            return
        sem, val, prod = ev
        key = (eng, sem)
        if self.waited.get(key, 0) >= val:
            return
        s = self.sems[sem]
        self.q[eng].append(lambda e, s=s, val=val: e.wait_ge(s, val))
        self.waited[key] = val

    def waits(self, eng, *evs):
        for ev in evs:
            self.wait(eng, ev)

    def barrier_all(self):
        evs = []
        for eng, sem in (("t", "t"), ("s", "s"), ("v", "v"), ("g", "g")):
            evs.append(self.emit(eng, lambda e: e.nop(), sem=sem))
        for eng in self.ENGS:
            for ev in evs:
                self.wait(eng, ev)
        return evs

    def flush(self, blk):
        for attr, eng in (("sync", "sy"), ("gpsimd", "g"), ("tensor", "t"),
                          ("scalar", "s"), ("vector", "v")):
            thunks = self.q[eng]

            def runner(e, thunks=thunks):
                for f in thunks:
                    f(e)
            getattr(blk, attr)(runner)

    def final_wait_all_dma(self):
        for eng in ("sy",):
            for sem in ("dx", "dw"):
                s, val = self.sems[sem], self.counts[sem]
                if val:
                    self.q[eng].append(lambda e, s=s, val=val: e.wait_ge(s, val))


def build_nc(upto_phase=5, debug=False, enddump=False):
    nc = Bacc("TRN2", target_bir_lowering=False, debug=False)
    dt = nc.dram_tensor

    x_in = dt("x_in", [128, DC, TCORE], F32, kind="ExternalInput").ap()
    wq_in = dt("wq_in", [128, DC, KD], F32, kind="ExternalInput").ap()
    wk_in = dt("wk_in", [128, DC, KD], F32, kind="ExternalInput").ap()
    wv1_in = dt("wv1_in", [128, DC, KD], F32, kind="ExternalInput").ap()
    wv2_in = dt("wv2_in", [KD, D], F32, kind="ExternalInput").ap()
    routw_in = dt("routw_in", [128, DC, 8], F32, kind="ExternalInput").ap()
    eb_in = dt("eb_in", [8, 1], F32, kind="ExternalInput").ap()
    sw1_in = dt("sw1_in", [128, DC, HD], F32, kind="ExternalInput").ap()
    sb1_in = dt("sb1_in", [128, 4], F32, kind="ExternalInput").ap()
    sw2_in = dt("sw2_in", [128, 4, D], F32, kind="ExternalInput").ap()
    b2_in = dt("b2_in", [9, DC, 128], F32, kind="ExternalInput").ap()
    rw1_in = dt("rw1_in", [8, 128, DC, HD], F32, kind="ExternalInput").ap()
    rb1_in = dt("rb1_in", [8, 128, 4], F32, kind="ExternalInput").ap()
    rw2_in = dt("rw2_in", [8, 128, 4, D], F32, kind="ExternalInput").ap()
    wmod_in = dt("wmod_in", [128, DC, 768], F32, kind="ExternalInput").ap()
    cvec_in = dt("cvec_in", [1, D], F32, kind="ExternalInput").ap()
    btab_in = dt("btab_in", [128, 2 * NCH], F32, kind="ExternalInput").ap()
    jtab_in = dt("jtab_in", [1, 60], I32, kind="ExternalInput").ap()
    wtab_in = dt("wtab_in", [65, 16], F32, kind="ExternalInput").ap()
    wtbb_in = dt("wtbb_in", [65, 16], F32, kind="ExternalInput").ap()
    y_out = dt("y_out", [128, DC, TCORE], F32, kind="ExternalOutput").ap()

    rg = [[0, 1, 2, 3, 4, 5, 6, 7]]
    asp = maybe_share_collective_output_space("AllGather", rg, prefer_local=False)
    ccm_in = dt("ccm_in", [6, 128], F32, kind="Internal").ap()
    ccm_out = dt("ccm_out", [NCORES, 6, 128], F32, kind="Internal",
                 addr_space=asp).ap()
    NKU = KD * TCORE + 128 * 1024
    cku_in = dt("cku_in", [NKU], BF16, kind="Internal").ap()
    cku_out = dt("cku_out", [NCORES, NKU], BF16, kind="Internal",
                 addr_space=asp).ap()
    ccd1_in = dt("ccd1_in", [1, 8], F32, kind="Internal").ap()
    ccd1_out = dt("ccd1_out", [NCORES, 1, 8], F32, kind="Internal",
                  addr_space=asp).ap()
    ccd2_in = dt("ccd2_in", [1, 8], F32, kind="Internal").ap()
    ccd2_out = dt("ccd2_out", [NCORES, 1, 8], F32, kind="Internal",
                  addr_space=asp).ap()
    gcd = dt("gcd", [2, D], F32, kind="Internal").ap()

    dbg = {}
    if debug:
        dbg["mod"] = dt("dbg_mod", [1, 6 * D], F32, kind="ExternalOutput").ap()
        dbg["q"] = dt("dbg_q", [KD, TCORE], F32, kind="ExternalOutput").ap()
        dbg["k"] = dt("dbg_k", [KD, T], F32, kind="ExternalOutput").ap()
        dbg["u"] = dt("dbg_u", [128, NCH * 512], F32, kind="ExternalOutput").ap()
        dbg["x1"] = dt("dbg_x1", [128, DC, TCORE], F32, kind="ExternalOutput").ap()
        dbg["h2"] = dt("dbg_h2", [128, DC, TCORE], F32, kind="ExternalOutput").ap()
        dbg["cf"] = dt("dbg_cf", [9, TCORE], F32, kind="ExternalOutput").ap()
        dbg["r1"] = dt("dbg_r1", [2, TCORE], F32, kind="ExternalOutput").ap()
        dbg["atm"] = dt("dbg_atm", [128, 128], F32, kind="ExternalOutput").ap()
        dbg["ctm"] = dt("dbg_ctm", [128, 128], F32, kind="ExternalOutput").ap()
        dbg["t2d"] = dt("dbg_t2d", [128, 32], F32, kind="ExternalOutput").ap()

    ctx = ExitStack()
    sb = lambda name, shape, dtp: nc.alloc_sbuf_tensor(name, shape, dtp).ap()
    psum = lambda name, shape, dtp=F32: nc.alloc_psum_tensor(name, shape, dtp).ap()

    x_fm = sb("x_fm", [128, DC, TCORE], F32)
    q_loc = sb("q_loc", [KD, NPAN, PW], BF16)
    k_loc = sb("k_loc", [KD, TCORE], BF16)
    u_tm_loc = sb("u_tm_loc", [128, 16, KD], BF16)
    wq_s = sb("wq_s", [128, DC, KD], BF16)
    wk_s = sb("wk_s", [128, DC, KD], BF16)
    wv1_s = sb("wv1_s", [128, DC, KD], BF16)
    wv2_s = sb("wv2_s", [KD, D], BF16)
    routw_s = sb("routw_s", [128, DC, 8], BF16)
    eb_s = sb("eb_s", [8, 1], F32)
    btab_s = sb("btab_s", [128, 2 * NCH], F32)
    jtab_s = sb("jtab_s", [1, 60], I32)
    wtab_s = sb("wtab_s", [65, 16], F32)
    wtab_b = sb("wtab_b", [65, 16], F32)
    ident = sb("ident", [128, 128], F32)
    ones_col = sb("ones_col", [128, 1], BF16)
    ones_row = sb("ones_row", [1, 128], BF16)
    sel9 = sb("sel9", [9, 9, 128], BF16)
    one_1x1 = sb("one_1x1", [1, 1], BF16)
    one_1x1f = sb("one_1x1f", [1, 1], F32)
    mone_row = sb("mone_row", [1, PW], BF16)
    g1_bf = sb("g1_bf", [1, D], BF16)
    sa_bf = sb("sa_bf", [1, D], BF16)
    g2_bf = sb("g2_bf", [1, D], BF16)
    sbv_bf = sb("sbv_bf", [1, D], BF16)
    ga_col = sb("ga_col", [128, DC], F32)
    gb_col = sb("gb_col", [128, DC], F32)
    att_f = sb("att_f", [65, PW], F32)
    warm = sb("warm", [1, 8], F32)
    att_b = sb("att_b", [65, PW], BF16)
    rec_p0 = sb("rec_p0", [1, PW], BF16)
    rb_s = sb("rb_s", [KD, PW], BF16)
    o64n = sb("o64n", [KD, PW], BF16)
    u_fm_s = sb("u_fm_s", [KD, PW], F32)

    POOLB = 102464
    pool_base, _pool_end = nc.bump_sbuf(POOLB + 32)
    pool_base = (pool_base + 31) & ~31
    _pvn = [0]

    def pv(off, p, n, dtp, inner=None):
        _pvn[0] += 1
        shape = [p, n] if inner is None else [p, n // inner, inner]
        h = nc.alloc_sbuf_tensor_at(f"pv{_pvn[0]}", shape, dtp,
                                    offset=pool_base + off)
        return h.ap()

    wmod_s = pv(0, 128, DC * 768, F32, 768)
    mod_row = pv(24576, 1, 6 * D, F32)
    msc_row = pv(49152, 1, 2 * D, F32)
    sc_f = pv(57344, 1, D, F32)
    sc_col = pv(61440, 128, DC, F32)

    k_gath = pv(0, KD, T, BF16, 1024)
    u_gath = pv(32768, 128, NCH * 512, BF16, 512)
    p_t = pv(58432, 128, 4 * PW, BF16, PW)
    p2_t = pv(58432, 128, 2 * 2 * PW, BF16, 2 * PW)
    uslot = pv(51200, 128, 2 * 8 * 65, BF16, 8 * 65)
    uslot_d = pv(53280, 128, 8 * 65, BF16)
    masks = pv(54336, 128, 4 * PW, BF16, PW)

    h2 = pv(0, 128, DC * TCORE, BF16, TCORE)
    FB = 32768
    w1s = [pv(FB + i * 16896, 128, DC * HD, BF16, HD) for i in range(2)]
    w2s = [pv(FB + 8192 + i * 16896, 128, 4 * D, BF16, D) for i in range(2)]
    b1s = [pv(FB + 16384 + i * 16896, 128, 4, F32) for i in range(2)]
    FB2 = FB + 2 * 16896
    hdn = pv(FB2, 128, 4 * TCORE, BF16, TCORE)
    cfb_t = pv(FB2 + 16384, 128, 4 * PW, BF16, PW)
    cf_rows = pv(FB2 + 20480, 9, TCORE, BF16)
    act_tm = pv(FB2 + 24576, 128, 16 * 8, F32, 8)
    cf_tm = pv(FB2 + 25088, 128, 16 * 8, F32, 8)
    act_fl = pv(FB2 + 24576, 128, 16 * 8, F32)
    cf_fl = pv(FB2 + 25088, 128, 16 * 8, F32)
    tcols = pv(FB2 + 25600, 128, 8, F32)
    tc2 = pv(FB2 + 25632, 128, 32, F32)
    rt_s = pv(FB2 + 25632, 8, TCORE, F32)
    b2_s = pv(FB2 + 33824, 9, DC * 128, BF16)
    assert FB2 + 33824 + DC * 128 * 2 <= POOLB

    o64acc = pv(66560, 65, NPAN * PW, F32, PW)
    ks_slot = pv(74752, 64, 2 * 1024, BF16, 1024)
    qs_slot = pv(78848, 64, 2 * 2 * PW, BF16, PW)

    LNS = FB2
    xbf_t = pv(LNS, 128, 3 * PW, BF16, PW)
    xsq_t = pv(LNS + 3072, 128, 3 * PW, BF16, PW)
    h1_t = pv(LNS + 6144, 128, 3 * PW, BF16, PW)
    t_f32 = pv(LNS + 9216, 128, 2 * PW, F32, PW)
    rows_f = pv(LNS + 13312, 1, 4 * PW, F32, PW)
    rows_b = pv(LNS + 21504, 1, 4 * PW, BF16, PW)
    assert LNS + 21504 + 4 * PW * 2 <= FB2 + 25632

    B01 = psum("bank01", [128, 2 * PW])
    B56 = psum("bank56", [128, 2 * PW])
    B = [None] * 8
    B[0] = B01[:, 0:PW]
    B[1] = B01[:, PW:2 * PW]
    B[5] = B56[:, 0:PW]
    B[6] = B56[:, PW:2 * PW]
    for i in (2, 3, 4, 7):
        B[i] = psum(f"bank{i}", [128, PW])[:, :]

    P = Prog(nc, ctx)
    E = {}

    with ctx:
        blk = ctx.enter_context(nc.Block())

        E["x"] = P.dma("sy", x_fm[:, :, :], x_in)
        E["btab"] = P.dma("sy", btab_s[:, :], btab_in)
        E["jtab"] = P.dma("sy", jtab_s[:, :], jtab_in)
        E["wtab"] = P.dma("sy", wtab_s[:, :], wtab_in)
        E["wtbb"] = P.dma("sy", wtab_b[:, :], wtbb_in)
        E["eb"] = P.dma("sy", eb_s[:, :], eb_in)
        E["cvec"] = P.dma("sy", att_f[0:1, 0:PW], cvec_in[:, 0:PW])
        E["cvec2"] = P.dma("sy", att_f[32:33, 0:PW], cvec_in[:, PW:D])

        E["wmod"] = P.dma("g", wmod_s, wmod_in)
        E["wq"] = P.dma("g", wq_s[:, :, :], wq_in)
        E["wk"] = P.dma("g", wk_s[:, :, :], wk_in)
        E["wv1"] = P.dma("g", wv1_s[:, :, :], wv1_in)
        E["wv2"] = P.dma("g", wv2_s[:, :], wv2_in)
        E["routw"] = P.dma("g", routw_s[:, :, :], routw_in)

        def consts(g):
            g.memset(ones_col[:, :], 1.0)
            g.memset(ones_row[:, :], 1.0)
            g.memset(one_1x1[:, :], 1.0)
            g.memset(one_1x1f[:, :], 1.0)
            g.memset(mone_row[:, :], -1.0)
            g.memset(ident[:, :], 0.0)
            g.memset(sel9[:, :, :], 0.0)
            return g.drain()
        E["consts0"] = P.emit("g", consts, sem="g")
        P.wait("g", E["consts0"])

        def consts1(g):
            g.affine_select(out=ident[:, :], in_=ident[:, :],
                            compare_op=ALU.not_equal, fill=1.0, base=0,
                            pattern=[[-1, 128]], channel_multiplier=1)
            g.affine_select(out=sel9[:, :, :], in_=sel9[:, :, :],
                            compare_op=ALU.not_equal, fill=1.0, base=0,
                            pattern=[[-1, 9], [0, 128]],
                            channel_multiplier=1)
            return g.drain()
        E["consts"] = P.emit("g", consts1, sem="g")

        P.wait("s", E["consts0"])

        def warmup(s):
            for fn in (AF.Silu, AF.Square, AF.Sqrt, AF.Exp, AF.Ln):
                inst = s.activation(warm[0:1, 0:1], one_1x1f[0:1, 0:1], fn)
            return inst
        E["warm"] = P.emit("s", warmup, sem="s")

        P.waits("s", E["cvec"], E["cvec2"])
        E["scrow"] = P.emit(
            "s", lambda s: (s.activation(sc_f[0:1, 0:PW], att_f[0:1, 0:PW], AF.Silu),
                            s.activation(sc_f[0:1, PW:D], att_f[32:33, 0:PW], AF.Silu))[-1],
            sem="s")

        P.waits("t", E["scrow"], E["consts"])
        for c in range(DC):
            ev = P.emit("t", lambda t, c=c: t.matmul(
                B[0][:, c:c + 1], sc_f[0:1, 128 * c:128 * (c + 1)],
                one_1x1f[:, :], start=True, stop=True, skip_group_check=True),
                sem="t" if c == DC - 1 else None)
        E["sccol_ps"] = ev
        P.wait("v", E["sccol_ps"])
        E["sccol"] = P.emit("v", lambda v: v.tensor_copy(sc_col[:, :], B[0][:, 0:DC]),
                            sem="v")

        P.waits("t", E["sccol"], E["wmod"])
        for c in range(DC):
            P.emit("t", lambda t, c=c: t.matmul(
                B[1][0:1, 0:512], sc_col[:, c:c + 1], wmod_s[:, c, 0:512],
                start=(c == 0), stop=(c == DC - 1), skip_group_check=True))
            ev = P.emit("t", lambda t, c=c: t.matmul(
                B[2][0:1, 0:256], sc_col[:, c:c + 1], wmod_s[:, c, 512:768],
                start=(c == 0), stop=(c == DC - 1), skip_group_check=True),
                sem="t" if c == DC - 1 else None)
        E["modsl_ps"] = ev
        P.wait("v", E["modsl_ps"])
        E["modsl"] = P.emit(
            "v", lambda v: (v.tensor_copy(msc_row[0:1, 0:512], B[1][0:1, 0:512]),
                            v.tensor_copy(msc_row[0:1, 512:768], B[2][0:1, 0:256]))[-1],
            sem="v")

        P.wait("sy", E["modsl"])
        E["modst"] = P.dma("sy", ccm_in.rearrange("j p -> (j p)")[None, :],
                           msc_row[0:1, 0:768])
        P.wait("g", E["modst"])
        E["dummy1st"] = P.dma("g", ccd1_in, warm[0:1, 0:8])
        P.wait("g", E["dummy1st"])
        E["modag"] = P.emit("g", lambda g: g.collective_compute(
            "AllGather", ALU.bypass, replica_groups=rg,
            ins=[ccm_in], outs=[ccm_out]), sem="g")
        E["dummy1"] = P.emit("g", lambda g: g.collective_compute(
            "AllGather", ALU.bypass, replica_groups=rg,
            ins=[ccd1_in], outs=[ccd1_out]), sem="g")
        P.wait("sy", E["dummy1"])
        E["modrow"] = P.dma("sy", mod_row[0:1, :],
                            ccm_out.rearrange("r j p -> (r j p)")[None, :])

        def mod_mid():
            P.wait("v", E["modrow"])

            def modrows(v):
                v.tensor_scalar(g1_bf[0:1, :], mod_row[0:1, D:2 * D], 1.0, None, ALU.add)
                v.tensor_scalar(g2_bf[0:1, :], mod_row[0:1, 4 * D:5 * D], 1.0, None, ALU.add)
                v.tensor_copy(sa_bf[0:1, :], mod_row[0:1, 0:D])
                v.tensor_copy(sbv_bf[0:1, :], mod_row[0:1, 3 * D:4 * D])
                for off in (2 * D, 5 * D):
                    g = mod_row[0:1, off:off + D]
                    sgn = msc_row[0:1, 0:D]
                    mx = msc_row[0:1, D:2 * D]
                    v.tensor_single_scalar(sgn, g, 0.0, ALU.is_ge)
                    v.drain()
                    v.tensor_scalar(sgn, sgn, 2.0, -1.0, ALU.mult, ALU.add)
                    v.drain()
                    v.tensor_mul(mx, g, sgn)
                    v.drain()
                    v.tensor_scalar(mx, mx, 0.25, None, ALU.max)
                    v.drain()
                    inst = v.tensor_mul(g, sgn, mx)
                    v.drain()
                return inst
            E["modrows"] = P.emit("v", modrows, sem="v")

            P.wait("sy", E["modrows"])
            e1 = P.dma("sy", gcd[0:1, :], mod_row[0:1, 2 * D:3 * D])
            e2 = P.dma("sy", gcd[1:2, :], mod_row[0:1, 5 * D:6 * D])
            P.waits("sy", e1, e2)
            P.dma("sy", ga_col[:, :],
                  gcd[0].rearrange("(c p) -> p c", p=128),
                  allow_slow_non_contiguous=True)
            E["gcols"] = P.dma(
                "sy", gb_col[:, :],
                gcd[1].rearrange("(c p) -> p c", p=128),
                allow_slow_non_contiguous=True)
            if debug:
                P.dma("sy", dbg["mod"], mod_row[0:1, :])

        ln_phase(P, E, dict(
            mid=mod_mid,
            tag="ln1", B=B, x_fm=x_fm, xbf_t=xbf_t, xsq_t=xsq_t, h1_t=h1_t,
            t_f32=t_f32, rows_f=rows_f, rows_b=rows_b, ones_col=ones_col,
            mone_row=mone_row, g_bf=g1_bf, s_bf=sa_bf,
            pre=("x",), bpre=("modrows",),
            cons=dict(kind="qku", wq_s=wq_s, wk_s=wk_s, wv1_s=wv1_s,
                      q_loc=q_loc, k_loc=k_loc, u_fm_s=u_fm_s,
                      u_tm_loc=u_tm_loc, ident=ident),
        ))

        P.waits("sy", E["ln1_done"])
        NK = KD * TCORE
        E["kst"] = P.dma("sy", cku_in[0:NK].rearrange("(a b) -> a b", b=TCORE),
                         k_loc[:, :])
        E["ust"] = P.dma("sy", cku_in[NK:NKU].rearrange("(a b) -> a b", b=1024),
                         u_tm_loc.rearrange("p a b -> p (a b)"))
        P.waits("g", E["kst"], E["ust"])
        E["dummy2st"] = P.dma("g", ccd2_in, warm[0:1, 0:8])
        P.wait("g", E["dummy2st"])
        E["kuag"] = P.emit("g", lambda g: g.collective_compute(
            "AllGather", ALU.bypass, replica_groups=rg,
            ins=[cku_in], outs=[cku_out]), sem="g")
        E["dummy2"] = P.emit("g", lambda g: g.collective_compute(
            "AllGather", ALU.bypass, replica_groups=rg,
            ins=[ccd2_in], outs=[ccd2_out]), sem="g")
        P.wait("sy", E["dummy2"])
        for c in range(NCH):
            rk = min(c, 15 - c)
            hf = 0 if c < 8 else 1
            kblk = cku_out[rk, 0:NK].rearrange("(a b) -> a b", b=TCORE)
            ublk = cku_out[rk, NK:NKU].rearrange("(a b) -> a b", b=1024)
            P.dma("sy", k_gath[:, c, :],
                  kblk[:, 1024 * hf:1024 * (hf + 1)])
            ev = P.dma("sy", u_gath[:, c, :],
                       ublk[:, 512 * hf:512 * (hf + 1)])
        E["kuload"] = ev

        if debug:
            P.wait("sy", E["kuload"])
            P.wait("g", E["kuload"])
            P.dma("g", dbg["q"], q_loc[:, :])
            P.dma("g", dbg["k"], k_gath.rearrange("k a b -> k (a b)"))
            P.dma("g", dbg["u"], u_gath.rearrange("k a b -> k (a b)"))

        if upto_phase >= 2:
            attention_phase(P, E, dict(
                B=B, x_fm=x_fm, q_loc=q_loc, k_loc=k_loc, u_tm_loc=u_tm_loc,
                k_gath=k_gath, u_gath=u_gath, p_t=p_t, p2_t=p2_t,
                B01=B01, B56=B56, uslot=uslot, o64acc=o64acc,
                ks_slot=ks_slot, qs_slot=qs_slot, jtab_s=jtab_s,
                wtab_s=wtab_s, wtab_b=wtab_b,
                uslot_d=uslot_d, masks=masks, btab_s=btab_s, att_f=att_f,
                att_b=att_b, rec_p0=rec_p0, rb_s=rb_s, o64n=o64n,
                ones_row=ones_row, wv2_s=wv2_s, ga_col=ga_col))
            if debug:
                P.wait("sy", E["attn_done"])
                P.dma("sy", dbg["x1"], x_fm[:, :, :])

        if upto_phase >= 3:
            P.barrier_all()
            ln_phase(P, E, dict(
                tag="ln2", B=B, x_fm=x_fm, xbf_t=xbf_t, xsq_t=xsq_t, h1_t=h1_t,
                t_f32=t_f32, rows_f=rows_f, rows_b=rows_b, ones_col=ones_col,
                mone_row=mone_row, g_bf=g2_bf, s_bf=sbv_bf,
                pre=("attn_done",), bpre=("modrows",),
                cons=dict(kind="h2", h2=h2),
            ))
            if debug:
                P.wait("sy", E["ln2_done"])
                P.wait("g", E["ln2_done"])
                P.dma("g", dbg["h2"], h2)

        if upto_phase >= 4:
            routing_phase(P, E, dict(
                B=B, h2=h2, routw_s=routw_s, eb_s=eb_s, rt_s=rt_s,
                act_tm=act_tm, cf_tm=cf_tm, tcols=tcols, cf_rows=cf_rows,
                act_fl=act_fl, cf_fl=cf_fl, tc2=tc2,
                ident=ident))
            if debug:
                P.wait("sy", E["cf_done"])
                P.wait("g", E["cf_done"])
                P.dma("g", dbg["cf"], cf_rows[:, :])
                P.dma("g", dbg["atm"], act_fl[:, :])
                P.dma("g", dbg["ctm"], cf_fl[:, :])
                P.dma("g", dbg["t2d"], tc2[:, :])

        if upto_phase >= 5:
            ffn_phase(P, E, dict(
                B=B, x_fm=x_fm, h2=h2, hdn=hdn, cfb_t=cfb_t, cf_rows=cf_rows,
                w1s=w1s, w2s=w2s, b1s=b1s, b2_s=b2_s, gb_col=gb_col,
                ones_row=ones_row, sel9=sel9, sw1_in=sw1_in, sb1_in=sb1_in,
                sw2_in=sw2_in, b2_in=b2_in, rw1_in=rw1_in, rb1_in=rb1_in,
                rw2_in=rw2_in))

        final_ev = {1: "ln1_done", 2: "attn_done", 3: "ln2_done",
                    4: "cf_done", 5: "ffn_done"}[min(upto_phase, 5)]
        P.wait("sy", E[final_ev])
        P.dma("sy", y_out, x_fm[:, :, :])
        if enddump:
            ed = {}
            for nm, shape in [("ga", [128, DC]), ("gb", [128, DC]),
                              ("qv", [KD, TCORE]), ("cfr", [9, TCORE]),
                              ("h2v", [128, DC * TCORE]), ("modv", [1, 6 * D])]:
                ed[nm] = dt("ed_" + nm, shape, F32, kind="ExternalOutput").ap()
            P.wait("g", E[final_ev])
            P.dma("g", ed["ga"], ga_col[:, :])
            P.dma("g", ed["gb"], gb_col[:, :])
            P.dma("g", ed["qv"], q_loc[:, :])
            P.dma("g", ed["cfr"], cf_rows[:, :])
            P.dma("g", ed["h2v"], h2.rearrange("p a b -> p (a b)"))
            P.dma("g", ed["modv"], mod_row[0:1, :])
        P.final_wait_all_dma()

        P.flush(blk)

    nc.compile()
    return nc


def ln_phase(P, E, A):
    tag, B, x_fm = A["tag"], A["B"], A["x_fm"]
    cons = A["cons"]
    rows_f, rows_b = A["rows_f"], A["rows_b"]
    g_bf, s_bf = A["g_bf"], A["s_bf"]
    pre = [E[k] for k in A["pre"]]
    bpre_keys = A["bpre"]

    for pan in range(NPAN):
        sl = slice(PW * pan, PW * (pan + 1))
        pb = pan % 2
        for c in range(DC):
            j = pan * DC + c
            i3 = j % 3
            P.waits("v", *pre)
            P.waits("s", *pre)
            if j >= 3:
                P.wait("v", E[(tag, "statmm", j - 3)])
                P.wait("s", E[(tag, "statmm", j - 3)])
            E[(tag, "xbf", j)] = P.emit("v", lambda v, c=c, i3=i3, sl=sl: v.tensor_copy(
                A["xbf_t"][:, i3, :], x_fm[:, c, sl]), sem="v")
            E[(tag, "xsq", j)] = P.emit("s", lambda s, c=c, i3=i3, sl=sl: s.activation(
                A["xsq_t"][:, i3, :], x_fm[:, c, sl], AF.Square), sem="s")
            P.waits("t", E[(tag, "xbf", j)], E[(tag, "xsq", j)])
            if c == 0 and pan >= 1:
                P.wait("t", E[(tag, "statrd", pan - 1)])
                P.wait("t", E[(tag, "hB", pan * DC - 1)])
            P.emit("t", lambda t, c=c, i3=i3: t.matmul(
                B[3][0:1, :], A["ones_col"][:, :], A["xbf_t"][:, i3, :],
                start=(c == 0), stop=(c == DC - 1), skip_group_check=True))
            E[(tag, "statmm", j)] = P.emit("t", lambda t, c=c, i3=i3: t.matmul(
                B[7][0:1, :], A["ones_col"][:, :], A["xsq_t"][:, i3, :],
                start=(c == 0), stop=(c == DC - 1), skip_group_check=True),
                sem="t")

        P.wait("v", E[(tag, "statmm", pan * DC + DC - 1)])
        if pan >= 2:
            P.wait("v", E[(tag, "bc", (pan - 1) * DC - 1)])
        mu = rows_f[0:1, 0, :]
        e2 = rows_f[0:1, 1, :]
        sd = rows_f[0:1, 2, :]
        mur = rows_f[0:1, 3, :]
        r_bf = rows_b[0:1, 2 * pb + 0, :]
        mur_bf = rows_b[0:1, 2 * pb + 1, :]
        if pan >= 1:
            P.wait("v", E[(tag, "rows", pan - 1)])

        def rows1(v, mu=mu, e2=e2):
            v.tensor_scalar(mu, B[3][0:1, :], 1.0 / D, None, ALU.mult)
            v.tensor_scalar(e2, B[7][0:1, :], 1.0 / D, None, ALU.mult)
            return v.drain()
        E[(tag, "statrd", pan)] = P.emit("v", rows1, sem="v")

        def rows2(v, mu=mu, e2=e2, sd=sd):
            v.tensor_mul(sd, mu, mu)
            v.drain()
            v.tensor_sub(e2, e2, sd)
            v.drain()
            v.tensor_scalar(e2, e2, 1e-5, None, ALU.add)
            return v.drain()
        E[(tag, "var", pan)] = P.emit("v", rows2, sem="v")
        P.wait("s", E[(tag, "var", pan)])
        E[(tag, "sd", pan)] = P.emit("s", lambda s, sd=sd, e2=e2: s.activation(
            sd, e2, AF.Sqrt), sem="s")
        P.wait("v", E[(tag, "sd", pan)])

        def rows3(v, sd=sd, mu=mu, mur=mur, r_bf=r_bf, mur_bf=mur_bf):
            v.reciprocal(sd, sd)
            v.drain()
            v.tensor_copy(r_bf, sd)
            v.tensor_mul(mur, mu, sd)
            v.drain()
            v.tensor_copy(mur_bf, mur)
            return v.drain()
        E[(tag, "rows", pan)] = P.emit("v", rows3, sem="v")

        if pan == 0 and A.get("mid") is not None:
            A["mid"]()

        P.wait("t", E[(tag, "rows", pan)])
        for k in bpre_keys:
            P.wait("t", E[k])
        LAG2 = 2

        def emit_bc(c, pan=pan, sl=sl, r_bf=r_bf, mur_bf=mur_bf):
            j = pan * DC + c
            rb = 4 if c % 2 == 0 else 6
            mb = 5 if c % 2 == 0 else 7
            if j >= 2:
                P.wait("t", E[(tag, "hA", j - 2)])
                P.wait("t", E[(tag, "hB", j - 2)])
            P.emit("t", lambda t, c=c, rb=rb: t.matmul(
                B[rb][:, :], g_bf[0:1, 128 * c:128 * (c + 1)], r_bf,
                start=True, stop=True, skip_group_check=True))
            P.emit("t", lambda t, c=c, mb=mb: t.matmul(
                B[mb][:, :], g_bf[0:1, 128 * c:128 * (c + 1)], mur_bf,
                start=True, stop=False, skip_group_check=True))
            E[(tag, "bc", j)] = P.emit("t", lambda t, c=c, mb=mb: t.matmul(
                B[mb][:, :], s_bf[0:1, 128 * c:128 * (c + 1)], A["mone_row"][0:1, :],
                start=False, stop=True, skip_group_check=True), sem="t")

            P.wait("v", E[(tag, "bc", j)])
            if cons["kind"] == "qku" and j >= 3:
                P.wait("v", E[(tag, "hfree", j - 3)])
            tf = A["t_f32"][:, j % 2, :]
            rb2 = 4 if c % 2 == 0 else 6
            E[(tag, "hA", j)] = P.emit("v", lambda v, tf=tf, c=c, sl=sl, rb2=rb2:
                                       v.tensor_mul(tf, x_fm[:, c, sl], B[rb2][:, :]),
                                       sem="v")
            P.wait("v", E[(tag, "hA", j)])
            if cons["kind"] == "qku":
                hdst = A["h1_t"][:, j % 3, :]
            else:
                hdst = cons["h2"][:, c, sl]
            mb2 = 5 if c % 2 == 0 else 7
            E[(tag, "hB", j)] = P.emit("v", lambda v, hdst=hdst, tf=tf, mb2=mb2:
                                       v.tensor_sub(hdst, tf, B[mb2][:, :]), sem="v")

        def emit_qku(c, pan=pan):
            j = pan * DC + c
            i3 = j % 3
            P.wait("t", E[(tag, "hB", j)])
            if c == 0:
                P.waits("t", E["wq"], E["wk"], E["wv1"])
                if pan >= 1:
                    P.wait("t", E[(tag, "utmc", (pan - 1) * 4 + 3)])
                    P.wait("t", E[(tag, "qkuev", pan - 1)])
            h = A["h1_t"][:, i3, :]
            P.emit("t", lambda t, c=c, h=h: t.matmul(
                B[0][0:KD, :], cons["wq_s"][:, c, :], h,
                start=(c == 0), stop=(c == DC - 1), skip_group_check=True))
            P.emit("t", lambda t, c=c, h=h: t.matmul(
                B[1][0:KD, :], cons["wk_s"][:, c, :], h,
                start=(c == 0), stop=(c == DC - 1), skip_group_check=True))
            E[(tag, "hfree", j)] = P.emit("t", lambda t, c=c, h=h: t.matmul(
                B[2][0:KD, :], cons["wv1_s"][:, c, :], h,
                start=(c == 0), stop=(c == DC - 1), skip_group_check=True),
                sem="t")

        if cons["kind"] == "qku":
            for step in range(DC + LAG2):
                if step < DC:
                    emit_bc(step)
                if step >= LAG2:
                    emit_qku(step - LAG2)
        else:
            for c in range(DC):
                emit_bc(c)

        if cons["kind"] == "qku":
            P.wait("v", E[(tag, "hfree", pan * DC + DC - 1)])

            def qkuev(v, sl=sl, pan=pan):
                v.tensor_scalar(cons["q_loc"][:, pan, :], B[0][0:KD, :], 0.125,
                                None, ALU.mult)
                v.tensor_copy(cons["k_loc"][:, sl], B[1][0:KD, :])
                v.tensor_copy(cons["u_fm_s"][:, :], B[2][0:KD, :])
                return v.drain()
            E[(tag, "qkuev", pan)] = P.emit("v", qkuev, sem="v")

            for tt in range(4):
                i = pan * 4 + tt
                P.wait("t", E[(tag, "qkuev", pan)])
                if i >= 1:
                    P.wait("t", E[(tag, "utmc", i - 1)])
                E[(tag, "utp", i)] = P.emit("t", lambda t, tt=tt: t.transpose(
                    B[2][:, 0:KD], cons["u_fm_s"][:, 128 * tt:128 * (tt + 1)],
                    cons["ident"][0:KD, 0:KD]), sem="t")
                P.wait("v", E[(tag, "utp", i)])
                E[(tag, "utmc", i)] = P.emit("v", lambda v, i=i: v.tensor_copy(
                    cons["u_tm_loc"][:, i, :], B[2][:, 0:KD]), sem="v")

    if cons["kind"] == "qku":
        E["ln1_done"] = E[(tag, "utmc", NPAN * 4 - 1)]
    else:
        E["ln2_done"] = E[(tag, "hB", NPAN * DC - 1)]


def attention_phase(P, E, A):
    B, x_fm = A["B"], A["x_fm"]
    B01, B56 = A["B01"], A["B56"]

    P.wait("g", E["modsl_ps"])

    def mk_masks0(g):
        for k in range(4):
            g.memset(A["masks"][:, k, :], 1.0)
        for b in range(2):
            for tt in range(8):
                g.memset(A["uslot"][:, b, 65 * tt + 64:65 * tt + 65], 1.0)
        for tt in range(8):
            g.memset(A["uslot_d"][:, 65 * tt + 64:65 * tt + 65], 1.0)
        return g.drain()
    E["maskinit0"] = P.emit("g", mk_masks0, sem="g")
    P.wait("g", E["maskinit0"])

    def mk_masks(g):
        for k in range(4):
            g.affine_select(
                out=A["masks"][:, k, :], in_=A["masks"][:, k, :],
                compare_op=ALU.is_ge, fill=0.0, base=-128 * k,
                pattern=[[1, PW]], channel_multiplier=-1)
        return g.drain()
    E["maskinit"] = P.emit("g", mk_masks, sem="g")
    P.wait("g", E["ln1_done"])
    E["accz"] = P.emit("g", lambda g: g.memset(A["o64acc"][0:65, :, :], 0.0),
                       sem="g")

    NJOBS = 17
    units = []
    jobpan = {}
    for j in range(2):
        for pan, tts in ((0, [(tt, True) for tt in range(4)]),
                         (1, [(tt, tt >= 4) for tt in range(8)])):
            for tt, mk in tts:
                jobpan.setdefault((j, pan), []).append(len(units))
                units.append((j, "diag", j, pan, tt, mk))
    for j in range(2, NJOBS):
        for pan in range(2):
            for tt in range(8):
                jobpan.setdefault((j, pan), []).append(len(units))
                units.append((j, "full", None, pan, tt, False))
    nu = len(units)
    pan_seq = []
    for j in range(NJOBS):
        for pan in range(2):
            pan_seq.append((j, pan))
    pan_idx = {jp: i for i, jp in enumerate(pan_seq)}

    STB = (0, 1, 5, 6)
    LAG = 3
    qreg = {}

    def stage_job(j):
        P.waits("v", E["kuload"], E["jtab"])
        if j >= 4:
            lastu = jobpan[(j - 2, 1)][-1]
            P.wait("v", E[("avmm", lastu)])

        def kstage(v, j=j):
            cidx = v.value_load(A["jtab_s"][0:1, 4 * (j - 2):4 * (j - 2) + 1],
                                min_val=0, max_val=NCH - 1)
            qq = v.value_load(A["jtab_s"][0:1, 4 * (j - 2) + 2:4 * (j - 2) + 3],
                              min_val=0, max_val=2)
            v.tensor_copy(A["ks_slot"][0:64, j % 2, :],
                          A["k_gath"][0:64, cidx, :])
            for pan2 in range(2):
                v.tensor_copy(A["qs_slot"][0:64, 2 * (j % 2) + pan2, :],
                              A["q_loc"][:, qq + pan2, :])
            for t8 in range(8):
                inst = v.tensor_copy(
                    A["uslot"][:, j % 2, 65 * t8:65 * t8 + 64],
                    A["u_gath"][:, cidx, 64 * t8:64 * (t8 + 1)])
            return inst
        E[("kstage", j)] = P.emit("v", kstage, sem="v")

    def emit_S(ui):
        j, kindj, qts, pan, tt, masked = units[ui]
        stb = STB[ui % 4]
        if kindj == "diag" and qts == 1 and pan == 1 and tt == 0:
            stage_job(2)
        if kindj == "full" and pan == 0 and tt == 4 and j + 1 < NJOBS:
            stage_job(j + 1)
        if ui >= 4:
            P.wait("t", E[("expp", (ui - 4) // 2)])
        if kindj == "full":
            P.wait("t", E[("kstage", j)])
            klhs = A["ks_slot"][0:64, j % 2, 128 * tt:128 * (tt + 1)]
            qrhs = A["qs_slot"][0:64, 2 * (j % 2) + pan, :]
        else:
            P.wait("t", E["ln1_done"])
            klhs = A["k_loc"][:, 1024 * qts + 128 * tt:1024 * qts + 128 * (tt + 1)]
            qrhs = A["q_loc"][:, qts * 2 + pan, :]
        E[("smm", ui)] = P.emit("t", lambda t, klhs=klhs, stb=stb, qrhs=qrhs:
                                t.matmul(B[stb][:, :], klhs, qrhs,
                                         start=True, stop=True,
                                         skip_group_check=True), sem="t")

        if ui % 2 == 1:
            pi = ui // 2
            P.wait("s", E[("smm", ui)])
            if ui >= 4:
                P.wait("s", E[("avmm", ui - 4)])
            src_ps = B01 if pi % 2 == 0 else B56
            E[("expp", pi)] = P.emit(
                "s", lambda s, pi=pi, src_ps=src_ps:
                s.activation(A["p2_t"][:, pi % 2, :], src_ps[:, :], AF.Exp),
                sem="s")
            for uu in (ui - 1, ui):
                ju, kj2, q2, pan2, tt2, mk2 = units[uu]
                if not mk2:
                    continue
                P.waits("v", E[("expp", pi)], E["maskinit"])
                mkt = tt2 if pan2 == 0 else tt2 - 4
                E[("pmask", uu)] = P.emit(
                    "v", lambda v, pb4=uu % 4, mkt=mkt: v.tensor_mul(
                        A["p_t"][:, pb4, :], A["p_t"][:, pb4, :],
                        A["masks"][:, mkt, :]), sem="v")

    def emit_AV(ui):
        j, kindj, qts, pan, tt, masked = units[ui]
        pb2 = ui % 4
        ju = jobpan[(j, pan)]
        first = (ui == ju[0])
        last = (ui == ju[-1])
        ob = 2 + (pan_idx[(j, pan)] % 2)
        if masked:
            P.wait("t", E[("pmask", ui)])
        else:
            P.wait("t", E[("expp", ui // 2)])
        if kindj == "full":
            ulhs = A["uslot"][:, j % 2, 65 * tt:65 * (tt + 1)]
        else:
            P.wait("t", E[("udiag", j)])
            ulhs = A["uslot_d"][:, 65 * tt:65 * (tt + 1)]
        if first and pan_idx[(j, pan)] >= 2:
            P.wait("t", E[("accadd", pan_idx[(j, pan)] - 2)])
        E[("avmm", ui)] = P.emit(
            "t", lambda t, ulhs=ulhs, pb2=pb2, ob=ob, first=first, last=last:
            t.matmul(B[ob][0:65, :], ulhs, A["p_t"][:, pb2, :],
                     start=first, stop=last, skip_group_check=True), sem="t")
        if last:
            pidx = pan_idx[(j, pan)]
            P.waits("v", E[("avmm", ui)], E["accz"], E["wtab"], E["wtbb"])

            def accadd(v, j=j, kindj=kindj, qts=qts, pan=pan, ob=ob):
                if kindj == "diag":
                    dst = A["o64acc"][0:65, qts * 2 + pan, :]
                    return v.tensor_add(dst, dst, B[ob][0:65, :])
                dA = A["o64acc"][0:65, pan, :]
                dB = A["o64acc"][0:65, 2 + pan, :]
                v.scalar_tensor_tensor(
                    out=dA, in0=B[ob][0:65, :], scalar=A["wtab_s"][0:65, j - 2:j - 1],
                    in1=dA, op0=ALU.mult, op1=ALU.add)
                return v.scalar_tensor_tensor(
                    out=dB, in0=B[ob][0:65, :],
                    scalar=A["wtab_b"][0:65, j - 2:j - 1],
                    in1=dB, op0=ALU.mult, op1=ALU.add)
            E[("accadd", pidx)] = P.emit("v", accadd, sem="v")

    for qt in range(2):
        P.waits("v", E["maskinit"], E["ln1_done"])
        if qt == 1:
            P.wait("v", E[("avmm", jobpan[(0, 1)][-1])])

        def udiag(v, qt=qt):
            for tt in range(8):
                inst = v.tensor_copy(A["uslot_d"][:, 65 * tt:65 * tt + 64],
                                     A["u_tm_loc"][:, qt * 8 + tt, :])
            return inst
        E[("udiag", qt)] = P.emit("v", udiag, sem="v")
        if qt == 0:
            for step in range(len(units)):
                if step < nu:
                    emit_S(step)
                if step >= LAG:
                    emit_AV(step - LAG)
                if qt == 0 and step + 1 == jobpan[(1, 0)][0] + LAG:
                    break
    start_step = jobpan[(1, 0)][0] + LAG
    for step in range(start_step, nu + LAG):
        if step < nu:
            emit_S(step)
        if step >= LAG:
            emit_AV(step - LAG)
    E["accall"] = E[("accadd", len(pan_seq) - 1)]

    for seq in range(4):
        qt, pan = seq // 2, seq % 2
        gcol = qt * 1024 + pan * PW
        acc = A["o64acc"]
        P.waits("v", E["accall"])
        if seq >= 1:
            P.wait("v", E[("recp0", seq - 1)])

        def rec(v, seq=seq):
            v.reciprocal(A["att_f"][64:65, :], acc[64:65, seq, :])
            v.drain()
            v.tensor_copy(A["att_b"][64:65, :], A["att_f"][64:65, :])
            return v.drain()
        E[("rec", seq)] = P.emit("v", rec, sem="v")

        P.wait("sy", E[("rec", seq)])
        if seq >= 1:
            P.wait("sy", E[("recb", seq - 1)])
        E[("recp0", seq)] = P.dma("sy", A["rec_p0"][0:1, :], A["att_b"][64:65, :])

        P.wait("t", E[("recp0", seq)])
        if seq >= 1:
            P.wait("t", E[("o64n", seq - 1)])
            P.wait("t", E[("xupd", seq - 1, 6)])
        E[("recb", seq)] = P.emit("t", lambda t: t.matmul(
            B[4][0:KD, :], A["ones_row"][0:1, 0:KD], A["rec_p0"][0:1, :],
            start=True, stop=True, skip_group_check=True), sem="t")

        P.wait("v", E[("recb", seq)])
        if seq >= 1:
            P.wait("v", E[("v2", seq - 1, DC - 1)])

        def o64nf(v, seq=seq):
            v.tensor_copy(A["rb_s"][:, :], B[4][0:KD, :])
            v.drain()
            v.tensor_mul(A["o64n"][:, :], acc[0:64, seq, :], A["rb_s"][:, :])
            return v.drain()
        E[("o64n", seq)] = P.emit("v", o64nf, sem="v")

        P.waits("t", E[("o64n", seq)], E["wv2"])
        for c in range(DC):
            ab = 4 if c % 2 == 0 else 7
            if c >= 2:
                P.wait("t", E[("xupd", seq, c - 2)])
            elif seq >= 1:
                P.wait("t", E[("xupd", seq - 1, DC - 2 + c)])
            E[("v2", seq, c)] = P.emit("t", lambda t, ab=ab, c=c: t.matmul(
                B[ab][:, :], A["wv2_s"][:, 128 * c:128 * (c + 1)],
                A["o64n"][:, :], start=True, stop=True,
                skip_group_check=True), sem="t")
            P.wait("v", E[("v2", seq, c)])
            P.wait("v", E["gcols"])
            xs = x_fm[:, c, gcol:gcol + PW]
            E[("xupd", seq, c)] = P.emit("v", lambda v, ab=ab, c=c, xs=xs:
                                         v.scalar_tensor_tensor(
                                             out=xs, in0=B[ab][:, :],
                                             scalar=A["ga_col"][:, c:c + 1],
                                             in1=xs, op0=ALU.mult, op1=ALU.add),
                                         sem="v")

    E["attn_done"] = E[("xupd", 3, DC - 1)]


def routing_phase(P, E, A):
    B = A["B"]
    for pan in range(NPAN):
        lb = pan % 2
        sl = slice(PW * pan, PW * (pan + 1))
        P.waits("t", E["ln2_done"], E["routw"])
        if pan >= 2:
            P.wait("t", E[("sp", pan - 2)])
        for c in range(DC):
            ev = P.emit("t", lambda t, c=c, lb=lb, sl=sl: t.matmul(
                B[lb][0:8, :], A["routw_s"][:, c, :], A["h2"][:, c, sl],
                start=(c == 0), stop=(c == DC - 1), skip_group_check=True),
                sem="t" if c == DC - 1 else None)
        E[("logit", pan)] = ev

        P.waits("s", E[("logit", pan)], E["eb"])
        E[("spe", pan)] = P.emit("s", lambda s, lb=lb, sl=sl: s.activation(
            A["rt_s"][0:8, sl], B[lb][0:8, :], AF.Exp, bias=A["eb_s"][:, :]),
            sem="s")
        P.wait("v", E[("spe", pan)])
        E[("sp1", pan)] = P.emit("v", lambda v, sl=sl: v.tensor_scalar(
            A["rt_s"][0:8, sl], A["rt_s"][0:8, sl], 1.0, None, ALU.add), sem="v")
        P.wait("s", E[("sp1", pan)])

        def sp(s, sl=sl):
            s.activation(A["rt_s"][0:8, sl], A["rt_s"][0:8, sl], AF.Ln)
            s.drain()
            s.activation(A["rt_s"][0:8, sl], A["rt_s"][0:8, sl], AF.Sqrt)
            return s.drain()
        E[("sp", pan)] = P.emit("s", sp, sem="s")

        for tt in range(4):
            i = pan * 4 + tt
            P.wait("t", E[("sp", pan)])
            if i >= 1:
                P.wait("t", E[("acttmc", i - 1)])
            E[("acttp", i)] = P.emit("t", lambda t, pan=pan, tt=tt: t.transpose(
                B[2][:, 0:8],
                A["rt_s"][0:8, PW * pan + 128 * tt:PW * pan + 128 * (tt + 1)],
                A["ident"][0:8, 0:8]), sem="t")
            P.wait("v", E[("acttp", i)])
            E[("acttmc", i)] = P.emit("v", lambda v, i=i: v.tensor_copy(
                A["act_tm"][:, i, :], B[2][:, 0:8]), sem="v")

    P.wait("v", E[("acttmc", 15)])

    def top2all(v):
        a = A["act_tm"]
        scr = A["cf_tm"]
        af, sf = A["act_fl"], A["cf_fl"]
        t2 = A["tc2"]
        for i in range(16):
            v.reduce_max(t2[:, 2 * i:2 * i + 1], a[:, i, :], axis=AX)
        v.drain()
        for i in range(16):
            v.scalar_tensor_tensor(out=scr[:, i, :], in0=a[:, i, :],
                                   scalar=t2[:, 2 * i:2 * i + 1], in1=a[:, i, :],
                                   op0=ALU.subtract, op1=ALU.bypass)
        v.drain()
        v.tensor_single_scalar(sf[:, :], sf[:, :], 0.0, ALU.is_equal)
        v.drain()
        v.tensor_single_scalar(sf[:, :], sf[:, :], NEG, ALU.mult)
        v.drain()
        v.tensor_add(sf[:, :], sf[:, :], af[:, :])
        v.drain()
        for i in range(16):
            v.reduce_max(t2[:, 2 * i + 1:2 * i + 2], scr[:, i, :], axis=AX)
        v.drain()
        for i in range(16):
            v.scalar_tensor_tensor(out=scr[:, i, :], in0=a[:, i, :],
                                   scalar=t2[:, 2 * i + 1:2 * i + 2],
                                   in1=a[:, i, :],
                                   op0=ALU.subtract, op1=ALU.bypass)
        v.drain()
        v.tensor_single_scalar(sf[:, :], sf[:, :], 0.0, ALU.is_ge)
        v.drain()
        v.tensor_mul(sf[:, :], sf[:, :], af[:, :])
        return v.drain()
    E["top2a"] = P.emit("v", top2all, sem="v")

    for pan in range(NPAN):
        sl = slice(PW * pan, PW * (pan + 1))
        for tt in range(4):
            i = pan * 4 + tt
            P.wait("t", E["top2a"])
            if pan >= 1 and tt == 0:
                P.wait("t", E[("cfrow", pan - 1)])
            E[("cftp", i)] = P.emit("t", lambda t, tt=tt, i=i: t.transpose(
                B[3][0:8, 128 * tt:128 * (tt + 1)], A["cf_tm"][:, i, :],
                A["ident"][:, :]), sem="t")

        P.wait("v", E[("cftp", pan * 4 + 3)])
        if pan == 0:
            P.wait("g", E["ln2_done"])
            E["cfones"] = P.emit("g", lambda g: g.memset(A["cf_rows"][0:9, :], 1.0),
                                 sem="g")
            P.wait("v", E["cfones"])
        E[("cfrow", pan)] = P.emit("v", lambda v, sl=sl: v.tensor_copy(
            A["cf_rows"][0:8, sl], B[3][0:8, :]), sem="v")

    E["cf_done"] = E[("cfrow", NPAN - 1)]


def ffn_phase(P, E, A):
    B, x_fm, h2, hdn = A["B"], A["x_fm"], A["h2"], A["hdn"]

    def load_member(m):
        s = m % 2
        if m >= 2:
            P.wait("g", E[("w2last", m - 2)])
        else:
            P.wait("g", E["attn_done"])
        if m == 0:
            P.dma("g", A["w1s"][s], A["sw1_in"])
            P.dma("g", A["b1s"][s], A["sb1_in"])
            E[("wl", m)] = P.dma("g", A["w2s"][s], A["sw2_in"])
            E["b2l"] = P.dma("g", A["b2_s"].rearrange("p (c f) -> p c f", f=128),
                             A["b2_in"])
        else:
            e = m - 1
            P.dma("g", A["w1s"][s], A["rw1_in"][e])
            P.dma("g", A["b1s"][s], A["rb1_in"][e])
            E[("wl", m)] = P.dma("g", A["w2s"][s], A["rw2_in"][e])

    load_member(0)
    load_member(1)

    for m in range(9):
        s = m % 2
        if m >= 1:
            for pan in range(NPAN):
                cb = 4 + pan % 2
                P.waits("t", E["cf_done"], E["cfones"])
                prev = ("cfbev", m, pan - 2) if pan >= 2 else \
                       (("cfbev", m - 1, pan + 2) if m >= 2 else None)
                if prev and prev in E:
                    P.wait("t", E[prev])
                if m == 1 and pan < 2:
                    P.wait("t", E["attn_done"])
                E[("cfb", m, pan)] = P.emit("t", lambda t, cb=cb, m=m, pan=pan:
                                            t.matmul(B[cb][:, :],
                                                     A["sel9"][:, m - 1, :],
                                                     A["cf_rows"][0:9,
                                                                  PW * pan:PW * (pan + 1)],
                                                     start=True, stop=True,
                                                     skip_group_check=True), sem="t")
                P.wait("v", E[("cfb", m, pan)])
                if m >= 2:
                    P.wait("v", E[("cfm", m - 1, pan * 4 + 3)])
                E[("cfbev", m, pan)] = P.emit("v", lambda v, cb=cb, pan=pan:
                                              v.tensor_copy(A["cfb_t"][:, pan, :],
                                                            B[cb][:, :]), sem="v")

        P.wait("t", E[("wl", m)])
        if m >= 1:
            P.wait("t", E[("w2last", m - 1)])
        for pan in range(NPAN):
            for hc in range(4):
                idx = pan * 4 + hc
                wb = idx % 2
                if idx >= 2:
                    P.wait("t", E[("silu", m, idx - 2)])
                elif m == 0:
                    P.wait("t", E[("sp", 2 + idx)])
                for c in range(DC):
                    ev = P.emit("t", lambda t, s=s, c=c, hc=hc, pan=pan, wb=wb:
                                t.matmul(B[wb][:, :],
                                         A["w1s"][s][:, c, 128 * hc:128 * (hc + 1)],
                                         h2[:, c, PW * pan:PW * (pan + 1)],
                                         start=(c == 0), stop=(c == DC - 1),
                                         skip_group_check=True),
                                sem="t" if c == DC - 1 else None)
                E[("w1", m, idx)] = ev

                P.wait("s", E[("w1", m, idx)])
                E[("silu", m, idx)] = P.emit(
                    "s", lambda sc, s=s, hc=hc, pan=pan, wb=wb: sc.activation(
                        hdn[:, hc, PW * pan:PW * (pan + 1)], B[wb][:, :], AF.Silu,
                        bias=A["b1s"][s][:, hc:hc + 1]), sem="s")

                if m >= 1:
                    P.waits("v", E[("silu", m, idx)], E[("cfbev", m, pan)])
                    E[("cfm", m, idx)] = P.emit(
                        "v", lambda v, hc=hc, pan=pan: v.tensor_mul(
                            hdn[:, hc, PW * pan:PW * (pan + 1)],
                            hdn[:, hc, PW * pan:PW * (pan + 1)],
                            A["cfb_t"][:, pan, :]), sem="v")

        if m == 8:
            P.waits("t", E["b2l"], E["cfones"], E["cf_done"])
        for pan in range(NPAN):
            for hc in range(4):
                idx = pan * 4 + hc
                key = ("cfm", m, idx) if m >= 1 else ("silu", m, idx)
                P.wait("t", E[key])
        for pan in range(NPAN):
            for c in range(DC):
                idx = pan * DC + c
                rbk = 2 + idx % 2
                if idx >= 2:
                    P.wait("t", E[("xffn", m, idx - 2)])
                elif m == 0:
                    P.wait("t", E[("acttmc", 15)])
                    P.wait("t", E[("cfrow", 3)])
                else:
                    P.wait("t", E[("xffn", m - 1, NPAN * DC - 2 + idx)])
                for hc in range(4):
                    ev = P.emit("t", lambda t, s=s, hc=hc, c=c, pan=pan, rbk=rbk,
                                m=m: t.matmul(
                                    B[rbk][:, :],
                                    A["w2s"][s][:, hc, 128 * c:128 * (c + 1)],
                                    hdn[:, hc, PW * pan:PW * (pan + 1)],
                                    start=(hc == 0),
                                    stop=(hc == 3 and m != 8),
                                    skip_group_check=True),
                                sem="t" if (hc == 3 and m != 8) else None)
                if m == 8:
                    ev = P.emit("t", lambda t, c=c, pan=pan, rbk=rbk: t.matmul(
                        B[rbk][:, :], A["b2_s"][:, 128 * c:128 * (c + 1)],
                        A["cf_rows"][:, PW * pan:PW * (pan + 1)],
                        start=False, stop=True, skip_group_check=True), sem="t")
                E[("w2", m, idx)] = ev

                P.wait("v", E[("w2", m, idx)])
                P.wait("v", E["gcols"])
                xs = x_fm[:, c, PW * pan:PW * (pan + 1)]
                E[("xffn", m, idx)] = P.emit(
                    "v", lambda v, rbk=rbk, c=c, xs=xs: v.scalar_tensor_tensor(
                        out=xs, in0=B[rbk][:, :], scalar=A["gb_col"][:, c:c + 1],
                        in1=xs, op0=ALU.mult, op1=ALU.add), sem="v")
        E[("w2last", m)] = E[("w2", m, NPAN * DC - 1)]
        E[("w2done", m)] = E[("xffn", m, NPAN * DC - 1)]
        if m + 2 <= 8:
            load_member(m + 2)

    E["ffn_done"] = E[("xffn", 8, NPAN * DC - 1)]


_CACHE = {}


def _host_prep(inputs):
    x = np.asarray(inputs["L0_attn_x"], dtype=np.float32)[0]
    wq = np.asarray(inputs["L0_attn_w_q"], dtype=np.float32)
    wk = np.asarray(inputs["L0_attn_w_k"], dtype=np.float32)
    wv1 = np.asarray(inputs["L0_attn_w_v1"], dtype=np.float32)
    wv2 = np.asarray(inputs["L0_attn_w_v2"], dtype=np.float32)
    routw = np.asarray(inputs["L0_ffn_routing_W"], dtype=np.float32)
    eb = np.asarray(inputs["L0_ffn_expert_bias"], dtype=np.float32)
    sw1 = np.asarray(inputs["L0_ffn_sW1"], dtype=np.float32)
    sb1 = np.asarray(inputs["L0_ffn_sb1"], dtype=np.float32)
    sw2 = np.asarray(inputs["L0_ffn_sW2"], dtype=np.float32)
    sb2 = np.asarray(inputs["L0_ffn_sb2"], dtype=np.float32)
    rw1 = np.asarray(inputs["L0_ffn_rW1"], dtype=np.float32)
    rb1 = np.asarray(inputs["L0_ffn_rb1"], dtype=np.float32)
    rw2 = np.asarray(inputs["L0_ffn_rW2"], dtype=np.float32)
    rb2 = np.asarray(inputs["L0_ffn_rb2"], dtype=np.float32)
    cv = np.asarray(inputs["L0_c"], dtype=np.float32)
    wmod = np.asarray(inputs["L0_wmod"], dtype=np.float32)

    C = np.ascontiguousarray
    shared = dict(
        wq_in=C(wq.reshape(DC, 128, KD).transpose(1, 0, 2)),
        wk_in=C(wk.reshape(DC, 128, KD).transpose(1, 0, 2)),
        wv1_in=C(wv1.reshape(DC, 128, KD).transpose(1, 0, 2)),
        wv2_in=C(wv2),
        routw_in=C(routw.T.reshape(DC, 128, 8).transpose(1, 0, 2)),
        eb_in=C(eb.reshape(8, 1)),
        sw1_in=C(sw1.reshape(DC, 128, HD).transpose(1, 0, 2)),
        sb1_in=C(sb1.reshape(4, 128).T),
        sw2_in=C(sw2.reshape(4, 128, D).transpose(1, 0, 2)),
        b2_in=C(np.concatenate([rb2, sb2[None, :]], 0).reshape(9, DC, 128)),
        rw1_in=C(rw1.reshape(8, DC, 128, HD).transpose(0, 2, 1, 3)),
        rb1_in=C(rb1.reshape(8, 4, 128).transpose(0, 2, 1)),
        rw2_in=C(rw2.reshape(8, 4, 128, D).transpose(0, 2, 1, 3)),
        cvec_in=C(cv.reshape(1, D)),
    )
    per_core = []
    for i in range(NCORES):
        Ac, Bc = i, 15 - i
        toks = np.concatenate([x[1024 * Ac:1024 * (Ac + 1)],
                               x[1024 * Bc:1024 * (Bc + 1)]], axis=0)
        x_h = C(toks.T.reshape(DC, 128, TCORE).transpose(1, 0, 2))
        wmod_h = C(wmod[:, 768 * i:768 * (i + 1)].reshape(DC, 128, 768)
                   .transpose(1, 0, 2))
        btab = np.zeros((128, 2 * NCH), dtype=np.float32)
        for qt, ch in ((0, Ac), (1, Bc)):
            for c in range(NCH):
                btab[:, qt * NCH + c] = 0.0 if c < ch else NEG
        jobs = [(0, c) for c in range(Ac)] + [(1, c) for c in range(Bc)]
        assert len(jobs) == 15
        jtab = np.zeros((1, 60), dtype=np.int32)
        wta = np.zeros((65, 16), dtype=np.float32)
        wtb = np.zeros((65, 16), dtype=np.float32)
        for j, (qt, c) in enumerate(jobs):
            jtab[0, 4 * j + 0] = c
            jtab[0, 4 * j + 1] = c
            jtab[0, 4 * j + 2] = 2 * qt
            jtab[0, 4 * j + 3] = 2 * qt
            wta[:, j] = 1.0 if qt == 0 else 0.0
            wtb[:, j] = 1.0 if qt == 1 else 0.0
        d = dict(shared)
        d.update(x_in=x_h, wmod_in=wmod_h, btab_in=btab, jtab_in=jtab,
                 wtab_in=wta, wtbb_in=wtb)
        per_core.append(d)
    return per_core


def _reassemble(results):
    out = np.empty((1, T, D), dtype=np.float32)
    for i in range(NCORES):
        y = results[i]["y_out"]
        toks = np.ascontiguousarray(np.asarray(y).transpose(2, 1, 0)).reshape(TCORE, D)
        Ac, Bc = i, 15 - i
        out[0, 1024 * Ac:1024 * (Ac + 1)] = toks[:1024]
        out[0, 1024 * Bc:1024 * (Bc + 1)] = toks[1024:]
    return out


def kernel(**inputs):
    per_core = _host_prep(inputs)
    key = ("v1", DEBUG)
    if key not in _CACHE:
        _CACHE[key] = build_nc(upto_phase=5, debug=DEBUG)
    nc = _CACHE[key]
    res = run_bass_kernel_spmd(nc, per_core, core_ids=list(range(NCORES)))
    return _reassemble(res.results)
